# version 05
# baseline (speedup 1.0000x reference)
import numpy as np
import ml_dtypes
from contextlib import ExitStack

import concourse.bass as bass
import concourse.mybir as mybir
from concourse.bass_utils import run_bass_kernel_spmd

F32 = mybir.dt.float32
BF16 = mybir.dt.bfloat16
AF = mybir.ActivationFunctionType
ALU = mybir.AluOpType
AX = mybir.AxisListType
BF = ml_dtypes.bfloat16

H, DK, DV, D, HDV = 8, 128, 256, 1024, 2048
C = 128
TS = 512
NE, TOPK = 64, 8
EH = 256
SH = 2048
RSCALE = float(DK) ** -0.5


class Cfg:
    def __init__(self, p_pre=6144, n_own=2048, stop_after=None, debug=False,
                 micro=99):
        self.p_pre = p_pre
        self.n_own = n_own
        self.stop_after = stop_after
        self.debug = debug
        self.micro = micro


def _ffn_tsups(n_own):
    out, rem = [], n_own
    while rem >= 384:
        out.append(384)
        rem -= 384
    if rem:
        out.append(rem)
    return out


class Bld:
    ENG = ("t", "v", "s", "y", "g")

    def __init__(self, nc, es, dma_plan=None):
        self.nc = nc
        self.es = es
        self.dma_plan = dma_plan
        self.dma_calls = {}
        self.ops = {e: [] for e in self.ENG}
        self.sem = {}
        for e in ("t", "v", "s"):
            self._mksem(e)
        self.waited = {e: {} for e in self.ENG}
        self.dma_keys = set()

    def _mksem(self, name):
        h = self.es.enter_context(self.nc.semaphore("sem_" + name))
        self.sem[name] = [h, 0]
        return self.sem[name]

    AUTO_SER = ("v", "s")

    def emit(self, e, fn):
        self.ops[e].append(fn)

    def op(self, e, fn):
        if e in self.AUTO_SER:
            return self.opi(e, fn)
        self.emit(e, fn)
        return None

    def opi(self, e, fn):
        h, n = self.sem[e]
        if e in self.AUTO_SER and n > self.waited[e].get(e, 0):
            self.waited[e][e] = n
            self.emit(e, lambda E, h=h, c=n: E.wait_ge(h, c))
        self.sem[e][1] += 1
        self.emit(e, lambda E, fn=fn, h=h: fn(E).then_inc(h, 1))
        return (e, self.sem[e][1])

    def tok(self, e):
        return (e, self.sem[e][1])

    def wait(self, e, tok):
        if tok is None:
            return
        pe, cnt = tok
        if pe == e and e not in self.AUTO_SER:
            return
        if self.waited[e].get(pe, 0) >= cnt:
            return
        self.waited[e][pe] = cnt
        h = self.sem[pe][0]
        self.emit(e, lambda E, h=h, c=cnt: E.wait_ge(h, c))

    wait_force = wait

    def dma(self, qe, out, in_, skey):
        if skey not in self.sem:
            self._mksem(skey)
            self.dma_keys.add(skey)
        idx = self.dma_calls.get(skey, 0)
        self.dma_calls[skey] = idx + 1
        amt = 16
        if self.dma_plan is not None:
            amt = self.dma_plan[skey][idx]
        self.sem[skey][1] += amt
        h = self.sem[skey][0]
        self.emit(qe, lambda E, o=out, i=in_, h=h: E.dma_start(
            out=o, in_=i).then_inc(h, 16))
        return (skey, self.sem[skey][1])

    def vbounce(self, vtok=None):
        if vtok is None:
            vtok = self.tok("v")
        self.wait("s", vtok)
        ap = self._bounce_ap
        ts = self.opi("s", lambda E, ap=ap: E.copy(ap[0:1, 0:1], ap[0:1, 1:2]))
        self.wait("v", ts)
        return ts

    def barrier(self, engines=None):
        engines = engines or self.ENG
        toks = [("t", self.sem["t"][1]), ("v", self.sem["v"][1]),
                ("s", self.sem["s"][1])]
        toks += [(k, self.sem[k][1]) for k in sorted(self.dma_keys)]
        for e in engines:
            for tok in toks:
                self.wait(e, tok)


class PB:
    def __init__(self, t):
        self.t = t
        self.wtok = None
        self.rtoks = []
        self.open = False
        self.unsealed = False

    def sl(self, off, n, poff=0, pn=128):
        return self.t[poff:poff + pn, off:off + n]


def mm(b, pb, out, lhsT, rhs, start, stop, seal=False):
    if start:
        assert not pb.open, "interleaved accumulation groups in one PSUM bank"
        for tok in pb.rtoks:
            b.wait("t", tok)
        pb.rtoks = []
        pb.open = True
    if stop:
        pb.open = False
    pb.unsealed = True
    fn = lambda E, o=out, l=lhsT, r=rhs, st=start, sp=stop: E.matmul(
        o, l, r, start=st, stop=sp)
    if seal:
        pb.wtok = b.opi("t", fn)
        pb.unsealed = False
    else:
        b.op("t", fn)


def pbread(b, pb, e, fn):
    assert not pb.unsealed, "reading PSUM bank with unsealed writes"
    b.wait(e, pb.wtok)
    tok = b.opi(e, fn)
    pb.rtoks.append(tok)
    return tok



def _extract_dma_plan(nc):
    from concourse.bass_interp import CoreSim
    sim = CoreSim(nc, no_exec=True)
    st = sim._sim_state
    plan = {}
    for name in nc.inst_names():
        raw = nc.find_inst_by_name(name)
        if raw is None or type(raw).__name__ != "InstDMACopy":
            continue
        si = raw.sync_info
        if not si or not si.on_update:
            continue
        u = si.on_update[0]
        if not u.ant_name.startswith("sem_"):
            continue
        amt = st.sem_dry_run_update(u.id, u, raw) - st.sem_value(u.id)
        plan.setdefault(u.ant_name[4:], []).append(int(amt))
    return plan


def build_program(cfg):
    nc1, es1, b1 = _build(cfg, None)
    plan = _extract_dma_plan(nc1)
    for k, n in b1.dma_calls.items():
        got = len(plan.get(k, []))
        assert got == n, f"dma plan mismatch for {k}: {got} updates vs {n} calls"
    es1.close()
    nc2, es2, _ = _build(cfg, plan)
    return nc2, es2


def _build(cfg, dma_plan):
    nc = bass.Bass("TRN2", target_bir_lowering=False)
    es = ExitStack()

    P_PRE, N_OWN = cfg.p_pre, cfg.n_own
    n_pre_st = P_PRE // TS
    n_own_st = N_OWN // TS
    n_own_ch = N_OWN // C
    ffn_ts = _ffn_tsups(N_OWN)
    n_tiles = N_OWN // C

    dr = {}
    def din(name, shape, dt=F32):
        dr[name] = nc.dram_tensor(name, list(shape), dt, kind="ExternalInput").ap()
        return dr[name]

    x_pre = din("x_pre", [max(P_PRE, 1), D])
    x_own = din("x_own", [N_OWN, D])
    Wq = din("Wq", [D, H * DK]); Wk = din("Wk", [D, H * DK])
    Wv = din("Wv", [D, H * DV]); Wo = din("Wo", [HDV, D])
    Wg = din("Wg", [D, H]); bg = din("bg", [H, 1]); Wog = din("Wog", [D, HDV])
    rw = din("rw", [D, NE]); rb = din("rb", [1, NE])
    w1 = din("w1", [NE, D, EH]); b1T = din("b1T", [EH, NE])
    w2 = din("w2", [NE, EH, D]); b2 = din("b2", [NE, D])
    sw1 = din("sw1", [2, D, SH]); sb1T = din("sb1T", [SH, 2])
    sw2 = din("sw2", [2, SH, D]); sb2 = din("sb2", [2, D])
    c_I128b = din("c_I128b", [128, 128], BF16)
    c_I8f = din("c_I8f", [8, 8])
    c_sel8f = din("c_sel8f", [8, 8 * 128])
    c_sel8b = din("c_sel8b", [8, 8 * 128], BF16)
    c_iota64 = din("c_iota64", [64, 1])
    c_ones64b = din("c_ones64b", [64, 128], BF16)
    c_mask = din("c_mask", [128, 128])
    c_ones1b = din("c_ones1b", [1, 128], BF16)
    c_ones2b = din("c_ones2b", [2, 128], BF16)
    c_eps = din("c_eps", [128, 1])

    x1_sp = nc.dram_tensor("x1_sp", [N_OWN, D], F32, kind="Internal").ap()
    y_out = nc.dram_tensor("y", [N_OWN, D], F32, kind="ExternalOutput").ap()
    dbg = {}
    if cfg.debug:
        for nm, shp in [("dbg_S", [128, H * DV]), ("dbg_xln", [128, D]),
                        ("dbg_q", [128, TS]), ("dbg_k", [128, TS]),
                        ("dbg_v", [128, HDV]), ("dbg_og", [128, TS]),
                        ("dbg_la", [8, TS]), ("dbg_b", [8, TS]),
                        ("dbg_mix", [128, D]), ("dbg_probs", [128, NE]),
                        ("dbg_cw", [128, NE]), ("dbg_acc", [128, D]),
                        ("dbg_h", [128, EH]),
                        ("dbg_xa", [128, D]), ("dbg_fm", [128, TS]),
                        ("dbg_alpha", [8, TS]), ("dbg_stats", [128, 8])]:
            dbg[nm] = nc.dram_tensor(nm, shp, F32, kind="ExternalOutput").ap()

    b = Bld(nc, es, dma_plan)
    _dbgn = [0]

    def dkey():
        _dbgn[0] += 1
        return "dbg%d" % _dbgn[0]

    sb = lambda name, shape, dt: es.enter_context(nc.sbuf_tensor(name, list(shape), dt))

    pb = [PB(es.enter_context(nc.psum_tensor(f"pb{i}", [128, 512], F32)))
          for i in range(8)]
    iI128b = sb("iI128b", [128, 128], BF16)
    iones1b = sb("iones1b", [1, 128], BF16)
    iones2b = sb("iones2b", [2, 128], BF16)
    CCm = TS // C
    expB_rd = [None, None]
    y_sb = sb("y_sb", [128, D], F32)
    sqbuf = [sb("sq0", [128, D], F32), sb("sq1", [128, D], F32)]
    sq_rd = [None, None]
    st_s1 = sb("st_s1", [128, 8], F32)
    st_sq = sb("st_sq", [128, 8], F32)
    st_mean = sb("st_mean", [128, 8], F32)
    st_msq = sb("st_msq", [128, 8], F32)
    st_var = sb("st_var", [128, 8], F32)
    st_rstd = sb("st_rstd", [128, 8], F32)
    st_mr = sb("st_mr", [128, 8], F32)
    ieps = sb("ieps", [128, 1], F32)
    yn_sb = sb("yn_sb", [128, D], F32)
    xlnA = sb("xlnA", [128, D], BF16)
    xlnB = sb("xlnB", [128, D], BF16)
    bounce = sb("bounce", [1, 2], F32)
    b._bounce_ap = bounce

    b.dma("y", iI128b[:, :], c_I128b[:, :], "c0")
    b.dma("y", iones1b[:, :], c_ones1b[:, :], "c0")
    b.dma("y", iones2b[:, :], c_ones2b[:, :], "c0")
    tk_eps = b.dma("y", ieps[:, :], c_eps[:, :], "c0")
    for _t in (st_s1, st_sq, st_mean, st_msq, st_var, st_rstd, st_mr, bounce):
        b.opi("v", lambda E, _t=_t: E.memset(_t[:, :], 0.0))

    mix_es = ExitStack()
    msb = lambda name, shape, dt: mix_es.enter_context(nc.sbuf_tensor(name, list(shape), dt))
    Wv_sb = msb("Wv_sb", [128, 8, H * DV], BF16)
    Wg_sb = msb("Wg_sb", [128, 8, H], BF16)
    x_tm = msb("x_tm", [128, CCm, D], F32)
    iI8f = msb("iI8f", [8, 8], F32)
    isel8f = msb("isel8f", [8, 8 * 128], F32)
    isel8b = msb("isel8b", [8, 8 * 128], BF16)
    imask = msb("imask", [128, 128], F32)
    S_state = msb("S_state", [128, H * DV], F32)
    bg_sb = msb("bg_sb", [H, 1], F32)
    bg16 = msb("bg16", [H, 1], F32)
    zeros8 = msb("zeros8", [8, C], F32)
    alpha = msb("alpha", [8, TS], F32)
    la = msb("la", [8, TS], F32)
    bsc = msb("bsc", [8, TS], F32)
    ts2 = msb("ts2", [8, TS], F32)
    Rxb = msb("Rxb", [8, TS], BF16)
    Rxm = msb("Rxm", [8, TS], BF16)
    btm = msb("btm", [128, CCm * 8], F32)
    Dpre = [msb(f"Dpre{p}", [128, 128], F32) for p in range(2)]
    Dt = [msb(f"Dt{p}", [128, 128], BF16) for p in range(2)]
    smb = [msb(f"smb{p}", [128, 128], BF16) for p in range(2)]
    qtb = [msb(f"qtb{p}", [128, 128], BF16) for p in range(2)]
    khb = [msb(f"khb{p}", [128, 128], BF16) for p in range(2)]
    khT = [msb(f"khT{p}", [128, 128], BF16) for p in range(2)]
    Sbf = [msb(f"Sbf{p}", [128, DV], BF16) for p in range(2)]
    expB_sb = [msb(f"expB{p}", [128, 1], F32) for p in range(2)]
    outog = msb("outog", [128, 16, 128], BF16)
    xln_fm = msb("xln_fm", [128, 8, TS], BF16)
    k_fm = msb("k_fm", [128, H, TS], BF16)
    v_tm = msb("v_tm", [128, CCm, H * DV], BF16)

    b.dma("y", iI8f[:, :], c_I8f[:, :], "c0")
    b.dma("y", isel8f[:, :], c_sel8f[:, :], "c0")
    b.dma("y", isel8b[:, :], c_sel8b[:, :], "c0")
    b.dma("y", imask[:, :], c_mask[:, :], "c0")
    tk_bg = b.dma("y", bg_sb[:, :], bg[:, :], "c0")
    tk_consts = tk_bg
    tk_eps = tk_bg
    b.wait("s", tk_bg)
    tk_bg16 = b.opi("s", lambda E: E.mul(bg16[:, :], bg_sb[:, :], 1.0 / 16.0))
    b.opi("v", lambda E: E.memset(zeros8[:, :], 0.0))
    tk_S0 = b.opi("v", lambda E: E.memset(S_state[:, :], 0.0))
    tk_wv = b.dma("g", Wv_sb[:, :, :], Wv.rearrange("(a p) n -> p a n", p=128), "wv")
    tk_wg = b.dma("g", Wg_sb[:, :, :], Wg.rearrange("(a p) n -> p a n", p=128), "wv")

    pre_es = ExitStack()
    Wk_sb = pre_es.enter_context(nc.sbuf_tensor("Wk_sb", [128, 8, H * DK], BF16))
    tk_wk = b.dma("g", Wk_sb[:, :, :], Wk.rearrange("(a p) n -> p a n", p=128), "wv")
    b.wait("v", tk_wk)
    tk_wks = b.opi("v", lambda E: E.tensor_scalar_mul(
        Wk_sb[:, :, :], Wk_sb[:, :, :], RSCALE))

    xlnbuf = [xlnA, xlnB]

    def emit_ln1(x_src_3d, cc, st_off=0):
        o = st_off
        for j in range(cc):
            sqb = sqbuf[j % 2]
            b.opi("v", lambda E, j=j: E.reduce_sum(
                st_s1[:, o + j:o + j + 1], x_src_3d[:, j, :], axis=AX.X))
            b.wait("s", sq_rd[j % 2])
            tsq = b.opi("s", lambda E, j=j, sqb=sqb: E.activation(
                sqb[:, :], x_src_3d[:, j, :], AF.Square))
            b.wait("v", tsq)
            sq_rd[j % 2] = b.opi("v", lambda E, j=j, sqb=sqb: E.reduce_sum(
                st_sq[:, o + j:o + j + 1], sqb[:, :], axis=AX.X))
        b.wait("s", b.tok("v"))
        b.opi("s", lambda E, o=o, cc=cc: E.mul(
            st_mean[:, o:o + cc], st_s1[:, o:o + cc], 1.0 / D))
        tmsq = b.opi("s", lambda E, o=o, cc=cc: E.activation(
            st_msq[:, o:o + cc], st_mean[:, o:o + cc], AF.Square))
        b.wait("v", tmsq)
        tvar = b.opi("v", lambda E, o=o, cc=cc: E.scalar_tensor_tensor(
            st_var[:, o:o + cc], st_sq[:, o:o + cc], 1.0 / D, st_msq[:, o:o + cc],
            ALU.mult, ALU.subtract))
        b.wait("s", tvar)
        b.wait("s", tk_eps)
        b.op("s", lambda E, o=o, cc=cc: E.activation(
            st_msq[:, o:o + cc], st_var[:, o:o + cc], AF.Ln,
            bias=ieps[:, :], scale=1.0))
        tkr = b.opi("s", lambda E, o=o, cc=cc: E.activation(
            st_rstd[:, o:o + cc], st_msq[:, o:o + cc], AF.Exp, bias=0.0, scale=-0.5))
        b.wait("v", tkr)
        tmr = b.opi("v", lambda E, o=o, cc=cc: E.scalar_tensor_tensor(
            st_mr[:, o:o + cc], st_mean[:, o:o + cc], -1.0, st_rstd[:, o:o + cc],
            ALU.mult, ALU.mult))
        b.wait("s", tmr)
        return tmr

    def emit_norm_sub(x_src_3d, j, dst, st_off=0):
        return b.opi("s", lambda E, j=j, dst=dst: E.activation(
            dst[:, :], x_src_3d[:, j, :], AF.Identity,
            bias=st_mr[:, st_off + j:st_off + j + 1],
            scale=st_rstd[:, st_off + j:st_off + j + 1]))

    def emit_transpose_to_fm(src_bf, dst_fm, tok_src, tslice0):
        b.wait("t", tok_src)
        b.wait("t", tk_consts)
        for half in range(2):
            bank = pb[2 + half]
            for q in range(4):
                dc = half * 4 + q
                mm(b, bank, bank.sl(q * 128, 128),
                   src_bf[:, dc * 128:(dc + 1) * 128], iI128b[:, :],
                   start=True, stop=True, seal=(q == 3))
            pbread(b, bank, "s", lambda E, bank=bank, half=half: E.copy(
                dst_fm[:, half * 4:half * 4 + 4, tslice0:tslice0 + 128],
                bank.t[:, 0:512].rearrange("p (a n) -> p a n", a=4)))
        return b.tok("s")

    wbuf_last = {}

    def proj_fm_stream(dst_fm, wbuf, wsrc_ap, n_slabs, Tcur, evict, skey="w?"):
        slab_tok = [None] * n_slabs
        use_tok = [None] * n_slabs
        prev = wbuf_last.get(id(wbuf))
        for i in range(min(2, n_slabs)):
            b.wait("g", prev)
            slab_tok[i] = b.dma("g", wbuf[:, i % 2, :, :],
                                wsrc_ap(i).rearrange("(a p) n -> p a n", p=128),
                                skey + str(i % 2))
        for i in range(n_slabs):
            b.wait("t", slab_tok[i])
            nparts = [(n0, min(512, Tcur - n0)) for n0 in range(0, Tcur, 512)]
            for pi, (n0, nn) in enumerate(nparts):
                bank = pb[2 + (i + pi) % 2]
                for kc in range(8):
                    mm(b, bank, bank.sl(0, nn),
                       wbuf[:, i % 2, kc, :], xln_fm[:, kc, n0:n0 + nn],
                       start=(kc == 0), stop=(kc == 7), seal=(kc == 7))
                evict(bank, i, n0, nn)
            use_tok[i] = b.tok("t")
            if i + 2 < n_slabs:
                b.wait("g", use_tok[i])
                slab_tok[i + 2] = b.dma("g", wbuf[:, (i + 2) % 2, :, :],
                                        wsrc_ap(i + 2).rearrange("(a p) n -> p a n", p=128),
                                        skey + str(i % 2))
        wbuf_last[id(wbuf)] = b.tok("t")

    def proj_fm_resident(dst_fm, Wres, n_slabs, Tcur, evict, scale_tok=None):
        for i in range(n_slabs):
            if scale_tok is not None:
                b.wait("t", scale_tok)
            nparts = [(n0, min(512, Tcur - n0)) for n0 in range(0, Tcur, 512)]
            for pi, (n0, nn) in enumerate(nparts):
                bank = pb[2 + (i + pi) % 2]
                for kc in range(8):
                    mm(b, bank, bank.sl(0, nn),
                       Wres[:, kc, i * 128:(i + 1) * 128], xln_fm[:, kc, n0:n0 + nn],
                       start=(kc == 0), stop=(kc == 7), seal=(kc == 7))
                evict(bank, i, n0, nn)

    def emit_vproj(cc, Tcur):
        for c in range(cc):
            for cb4 in range(4):
                bank = pb[2 + cb4 % 2]
                for kc in range(8):
                    mm(b, bank, bank.sl(0, 512),
                       xln_fm[:, kc, c * 128:(c + 1) * 128],
                       Wv_sb[:, kc, cb4 * 512:(cb4 + 1) * 512],
                       start=(kc == 0), stop=(kc == 7), seal=(kc == 7))
                pbread(b, bank, "v", lambda E, bank=bank, c=c, cb4=cb4: E.tensor_copy(
                    v_tm[:, c, cb4 * 512:(cb4 + 1) * 512], bank.t[:, 0:512]))
        return b.tok("v")

    def emit_gproj(cc, Tcur):
        nparts = [(n0, min(512, Tcur - n0)) for n0 in range(0, Tcur, 512)]
        for pi, (n0, nn) in enumerate(nparts):
            bank = pb[6]
            for kc in range(8):
                mm(b, bank, bank.sl(0, nn, 0, 8),
                   Wg_sb[:, kc, :], xln_fm[:, kc, n0:n0 + nn],
                   start=(kc == 0), stop=(kc == 7), seal=(kc == 7))
            b.wait("s", tk_bg16)
            pbread(b, bank, "s", lambda E, bank=bank, n0=n0, nn=nn: E.activation(
                alpha[:, n0:n0 + nn], bank.sl(0, nn, 0, 8), AF.Sigmoid,
                bias=bg16[:, :], scale=1.0 / 16.0))
        tka = b.tok("s")
        b.wait("v", tka)
        tkc = b.opi("v", lambda E, T=Tcur: E.tensor_scalar_max(
            alpha[:, 0:T], alpha[:, 0:T], 1e-38))
        b.wait("s", tkc)
        return b.opi("s", lambda E, T=Tcur: E.activation(
            la[:, 0:T], alpha[:, 0:T], AF.Ln, bias=0.0, scale=1.0))

    def emit_gateprep(cc, tk_la):
        b.wait("v", tk_la)
        micro = cfg.micro
        tscan = None
        for c in range(cc):
            fn = lambda E, c=c: E.tensor_tensor_scan(
                bsc[:, c * C:(c + 1) * C], la[:, c * C:(c + 1) * C],
                zeros8[:, :], 0.0, ALU.add, ALU.add)
            if c == cc - 1:
                tscan = b.opi("v", fn)
            else:
                b.op("v", fn)
        if micro < 5:
            return None, None
        b.wait("s", tscan)
        tRb = b.opi("s", lambda E, cc=cc: E.activation(
            Rxb[:, 0:cc * C], bsc[:, 0:cc * C], AF.Exp, bias=0.0, scale=1.0))
        if micro < 6:
            return None, None
        b.wait("t", tscan)
        bank = pb[2]
        for c in range(cc):
            mm(b, bank, bank.sl(c * 8, 8), bsc[:, c * C:(c + 1) * C], iI8f[:, :],
               start=True, stop=True, seal=(c == cc - 1))
        pbread(b, bank, "s", lambda E, bank=bank, cc=cc: E.copy(
            btm[:, 0:cc * 8], bank.t[:, 0:cc * 8]))
        if micro < 7:
            return None, None
        b.wait("v", tRb)
        tts2 = None
        for c in range(cc):
            fn = lambda E, c=c: E.tensor_scalar(
                ts2[:, c * C:(c + 1) * C], bsc[:, c * C:(c + 1) * C],
                bsc[:, c * C + C - 1:c * C + C], -1.0, ALU.subtract, ALU.mult)
            if c == cc - 1:
                tts2 = b.opi("v", fn)
            else:
                b.op("v", fn)
        b.wait("s", tts2)
        tkR = b.opi("s", lambda E, cc=cc: E.activation(
            Rxm[:, 0:cc * C], ts2[:, 0:cc * C], AF.Exp, bias=0.0, scale=1.0))
        return tkR, None

    khb_rd = [None, None]

    def emit_prefix_scan(cc, tkR, v_tok):
        bank6, bank7 = pb[6], pb[7]
        seq = [(c, h) for c in range(cc) for h in range(H)]
        st_f = {}

        def front(i):
            c, h = seq[i]
            par = i % 2
            b.wait("t", tkR)
            mm(b, bank6, bank6.sl(par * 256, 128), isel8b[:, h * 128:(h + 1) * 128],
               Rxb[:, c * C:(c + 1) * C], start=True, stop=True)
            mm(b, bank6, bank6.sl(par * 256 + 128, 128), isel8b[:, h * 128:(h + 1) * 128],
               Rxm[:, c * C:(c + 1) * C], start=True, stop=True, seal=True)
            b.wait("v", khb_rd[par])
            tkh = pbread(b, bank6, "v", lambda E, par=par, c=c, h=h: E.tensor_tensor(
                khb[par][:, :], k_fm[:, h, c * C:(c + 1) * C],
                bank6.sl(par * 256 + 128, 128), ALU.mult))
            b.wait("s", expB_rd[par])
            b.wait("s", tkh)
            tEB = pbread(b, bank6, "s", lambda E, par=par: E.copy(
                expB_sb[par][:, :], bank6.sl(par * 256 + 127, 1)))
            st_f[i] = (tkh, tEB)

        def mid(i):
            par = i % 2
            tkh, tEB = st_f[i]
            b.wait("t", tkh)
            mm(b, bank7, bank7.sl(par * 128, 128), khb[par][:, :], iI128b[:, :],
               start=True, stop=True, seal=True)
            khb_rd[par] = ("t", b.sem["t"][1])
            tkt = pbread(b, bank7, "s", lambda E, par=par: E.copy(
                khT[par][:, :], bank7.sl(par * 128, 128)))
            st_f[i] = (tkt, tEB)

        def back(i):
            c, h = seq[i]
            par = i % 2
            tkt, tEB = st_f.pop(i)
            b.wait("t", tkt)
            if v_tok is not None:
                b.wait("t", v_tok)
            mm(b, bank7, bank7.sl(256, 256), khT[par][:, :],
               v_tm[:, c, h * DV:(h + 1) * DV], start=True, stop=True, seal=True)
            b.wait("v", tEB)
            tadd = pbread(b, bank7, "v", lambda E, par=par, h=h: E.scalar_tensor_tensor(
                S_state[:, h * DV:(h + 1) * DV], S_state[:, h * DV:(h + 1) * DV],
                expB_sb[par][:, :], bank7.sl(256, 256), ALU.mult, ALU.add))
            expB_rd[par] = tadd

        n = len(seq)
        for i in range(n):
            front(i)
            if i >= 1:
                mid(i - 1)
            if i >= 2:
                back(i - 2)
        if n >= 1:
            mid(n - 1)
        if n >= 2:
            back(n - 2)
        if n >= 1:
            back(n - 1)

    x_cons = [None]
    x_par = [0]

    def emit_prefix():
        for s in range(n_pre_st):
            if x_cons[0] is not None:
                b.wait("y", x_cons[0][0])
                b.wait("y", x_cons[0][1])
            tkx = b.dma("y", x_tm[:, :, :],
                        x_pre.rearrange("(a p) n -> p a n", p=128)[:, s * CCm:(s + 1) * CCm, :],
                        "x%d" % (x_par[0] % 2))
            x_par[0] += 1
            b.wait("v", tkx)
            b.wait("s", tkx)
            emit_ln1(x_tm, CCm)
            for j in range(CCm):
                tkn = emit_norm_sub(x_tm, j, xlnbuf[j % 2])
                emit_transpose_to_fm(xlnbuf[j % 2], xln_fm, tkn, j * 128)
            tk_fmdone = b.tok("s")
            x_cons[0] = (b.tok("v"), b.tok("s"))
            if cfg.micro < 2:
                continue
            b.wait("t", tk_fmdone)
            b.wait("t", tk_wk)
            b.wait("t", tk_wv)
            b.wait("t", tk_wg)
            def kev(bank, i, n0, nn):
                pbread(b, bank, "v", lambda E, bank=bank, i=i, n0=n0, nn=nn: E.tensor_copy(
                    k_fm[:, i, n0:n0 + nn], bank.sl(0, nn)))
            proj_fm_resident(k_fm, Wk_sb, H, TS, kev, scale_tok=tk_wks)
            v_tok = emit_vproj(CCm, TS)
            if cfg.micro < 3:
                x_cons[0] = (b.tok("v"), b.tok("s"))
                continue
            tk_la = emit_gproj(CCm, TS)
            x_cons[0] = (b.tok("v"), b.tok("s"))
            if cfg.micro < 4:
                continue
            tkR, _ = emit_gateprep(CCm, tk_la)
            x_cons[0] = (b.tok("v"), b.tok("s"))
            if cfg.micro < 8:
                continue
            emit_prefix_scan(CCm, tkR, v_tok)
        if cfg.debug:
            tkf = b.tok("v")
            b.wait("y", tkf)
            tdS = b.dma("y", dbg["dbg_S"][:, :], S_state[:, :], dkey())
            for _e in ("t", "v", "s"):
                b.wait(_e, tdS)

    def emit_own():
        own_es = ExitStack()
        osb = lambda name, shape, dt: own_es.enter_context(
            nc.sbuf_tensor(name, list(shape), dt))
        Wo_sb = osb("Wo_sb", [128, 16, D], BF16)
        q_fm = osb("q_fm", [128, H, TS], BF16)
        og_fm = osb("og_fm", [128, 16, TS], BF16)
        wq_buf = osb("wq_buf", [128, 2, 8, 128], BF16)
        wk_buf = osb("wk_buf", [128, 2, 8, 128], BF16)
        wog_buf = osb("wog_buf", [128, 2, 8, 128], BF16)

        tk_wo = b.dma("g", Wo_sb[:, :, :], Wo.rearrange("(a p) n -> p a n", p=128), "wo")
        x1_store = [None]
        dpre_rd = {0: None, 1: None}

        for s in range(n_own_st):
            if x_cons[0] is not None:
                b.wait("y", x_cons[0][0])
                b.wait("y", x_cons[0][1])
            if x1_store[0] is not None:
                b.wait_force("y", x1_store[0])
            tkx = b.dma("y", x_tm[:, :, :],
                        x_own.rearrange("(a p) n -> p a n", p=128)[:, s * CCm:(s + 1) * CCm, :],
                        "x%d" % (x_par[0] % 2))
            x_par[0] += 1
            b.wait("v", tkx)
            b.wait("s", tkx)
            emit_ln1(x_tm, CCm)
            for j in range(CCm):
                tkn = emit_norm_sub(x_tm, j, xlnbuf[j % 2])
                emit_transpose_to_fm(xlnbuf[j % 2], xln_fm, tkn, j * 128)
                if cfg.debug and s == 0 and j == 0:
                    tsn = b.tok("v")
                    b.wait("y", tsn)
                    tdd = b.dma("y", dbg["dbg_xln"][:, :], x_tm[:, 0, :], dkey())
                    for _e in ("t", "v", "s"):
                        b.wait(_e, tdd)
                    b.wait("g", tsn)
                    b.wait("g", b.tok("s"))
                    tdx = b.dma("g", dbg["dbg_xa"][:, :], xlnbuf[0][:, :], dkey())
                    for _e in ("t", "v", "s"):
                        b.wait(_e, tdx)

            tk_fmdone = b.tok("s")
            b.wait("t", tk_fmdone)
            def qev(bank, i, n0, nn):
                pbread(b, bank, "v", lambda E, bank=bank, i=i, n0=n0, nn=nn: E.tensor_copy(
                    q_fm[:, i, n0:n0 + nn], bank.sl(0, nn)))
            proj_fm_stream(q_fm, wq_buf, lambda i: Wq[:, i * 128:(i + 1) * 128],
                           H, TS, qev, skey="wq")
            def kev(bank, i, n0, nn):
                pbread(b, bank, "v", lambda E, bank=bank, i=i, n0=n0, nn=nn: E.tensor_scalar(
                    k_fm[:, i, n0:n0 + nn], bank.sl(0, nn), RSCALE, None, ALU.mult))
            proj_fm_stream(k_fm, wk_buf, lambda i: Wk[:, i * 128:(i + 1) * 128],
                           H, TS, kev, skey="wk")
            v_tok = emit_vproj(CCm, TS)
            def ogev(bank, i, n0, nn):
                pbread(b, bank, "s", lambda E, bank=bank, i=i, n0=n0, nn=nn: E.activation(
                    og_fm[:, i, n0:n0 + nn], bank.sl(0, nn), AF.Sigmoid,
                    bias=0.0, scale=1.0))
            proj_fm_stream(og_fm, wog_buf, lambda i: Wog[:, i * 128:(i + 1) * 128],
                           16, TS, ogev, skey="wg")
            tok_og = b.tok("s")
            if cfg.debug and s == 0:
                tfm = b.tok("s")
                b.wait("g", tfm)
                tdf = b.dma("g", dbg["dbg_fm"][:, :], xln_fm[:, 0, :], dkey())
                for _e in ("t", "v", "s"):
                    b.wait(_e, tdf)
            tk_la = emit_gproj(CCm, TS)
            if cfg.debug and s == 0:
                b.wait("y", tk_la)
                tda = b.dma("y", dbg["dbg_alpha"][:, :], alpha[:, :], dkey())
                for _e in ("t", "v", "s"):
                    b.wait(_e, tda)
            tkR, _ = emit_gateprep(CCm, tk_la)
            x_cons[0] = (b.tok("v"), b.tok("s"))
            b.wait("v", tok_og)
            if cfg.debug and s == 0:
                tkq = b.tok("v")
                b.wait("g", tkq)
                b.dma("g", dbg["dbg_q"][:, :], q_fm[:, 0, :], dkey())
                b.dma("g", dbg["dbg_k"][:, :], k_fm[:, 0, :], dkey())
                tdq = b.dma("g", dbg["dbg_v"][:, :], v_tm[:, 0, :], dkey())
                for _e in ("t", "v", "s"):
                    b.wait(_e, tdq)
                tog = b.tok("s")
                b.wait("g", tog)
                b.dma("g", dbg["dbg_og"][:, :], og_fm[:, 0, :], dkey())
                b.dma("g", dbg["dbg_la"][:, :], la[:, :], dkey())
                tdo = b.dma("g", dbg["dbg_b"][:, :], bsc[:, :], dkey())
                for _e in ("t", "v", "s"):
                    b.wait(_e, tdo)

            for c in range(CCm):
                ci = s * CCm + c
                for h in range(H):
                    par = h % 2
                    bank6 = pb[6]
                    b.wait("t", tkR)
                    b.wait("v", dpre_rd[par])
                    mm(b, bank6, bank6.sl(par * 256, 128),
                       isel8b[:, h * 128:(h + 1) * 128],
                       Rxb[:, c * C:(c + 1) * C], start=True, stop=True)
                    mm(b, bank6, bank6.sl(par * 256 + 128, 128),
                       isel8b[:, h * 128:(h + 1) * 128],
                       Rxm[:, c * C:(c + 1) * C], start=True, stop=True, seal=True)
                    bank4 = pb[4 + par]
                    mm(b, bank4, bank4.sl(384, 128), isel8f[:, h * 128:(h + 1) * 128],
                       bsc[:, c * C:(c + 1) * C], start=True, stop=True, seal=True)
                    tdp = pbread(b, bank4, "v",
                                 lambda E, bank4=bank4, par=par, c=c, h=h: E.scalar_tensor_tensor(
                                     Dpre[par][:, :], bank4.sl(384, 128),
                                     btm[:, c * 8 + h:c * 8 + h + 1], imask[:, :],
                                     ALU.subtract, ALU.add))
                    b.wait("s", tdp)
                    tdt = b.opi("s", lambda E, par=par: E.activation(
                        Dt[par][:, :], Dpre[par][:, :], AF.Exp, bias=0.0, scale=1.0))
                    dpre_rd[par] = tdt
                    mm(b, bank4, bank4.sl(0, 128), k_fm[:, h, c * C:(c + 1) * C],
                       q_fm[:, h, c * C:(c + 1) * C], start=True, stop=True, seal=True)
                    b.wait("v", tdt)
                    tsm = pbread(b, bank4, "v", lambda E, bank4=bank4, par=par: E.tensor_tensor(
                        smb[par][:, :], bank4.sl(0, 128), Dt[par][:, :], ALU.mult))
                    tqt = pbread(b, bank6, "v", lambda E, bank6=bank6, par=par, c=c, h=h: E.tensor_tensor(
                        qtb[par][:, :], q_fm[:, h, c * C:(c + 1) * C],
                        bank6.sl(par * 256, 128), ALU.mult))
                    tkh = pbread(b, bank6, "v", lambda E, bank6=bank6, par=par, c=c, h=h: E.tensor_tensor(
                        khb[par][:, :], k_fm[:, h, c * C:(c + 1) * C],
                        bank6.sl(par * 256 + 128, 128), ALU.mult))
                    b.wait("s", expB_rd[par])
                    b.wait("s", tkh)
                    tEB = pbread(b, bank6, "s", lambda E, bank6=bank6, par=par: E.copy(
                        expB_sb[par][:, :], bank6.sl(par * 256 + 127, 1)))
                    tprev = b.tok("v")
                    b.wait("s", tprev)
                    b.wait("s", b.tok("t"))
                    tsb = b.opi("s", lambda E, par=par, h=h: E.copy(
                        Sbf[par][:, :], S_state[:, h * DV:(h + 1) * DV]))
                    b.wait("t", tsm)
                    b.wait("t", tqt)
                    b.wait("t", tsb)
                    b.wait("t", v_tok)
                    mm(b, bank4, bank4.sl(128, 128), v_tm[:, c, h * DV:h * DV + 128],
                       smb[par][:, :], start=True, stop=False)
                    mm(b, bank4, bank4.sl(128, 128), Sbf[par][:, 0:128],
                       qtb[par][:, :], start=False, stop=True)
                    mm(b, bank4, bank4.sl(256, 128), v_tm[:, c, h * DV + 128:h * DV + 256],
                       smb[par][:, :], start=True, stop=False)
                    mm(b, bank4, bank4.sl(256, 128), Sbf[par][:, 128:256],
                       qtb[par][:, :], start=False, stop=True, seal=True)
                    pbread(b, bank4, "v", lambda E, bank4=bank4, c=c, h=h: E.tensor_tensor(
                        outog[:, 2 * h, :], bank4.sl(128, 128),
                        og_fm[:, 2 * h, c * C:(c + 1) * C], ALU.mult))
                    pbread(b, bank4, "v", lambda E, bank4=bank4, c=c, h=h: E.tensor_tensor(
                        outog[:, 2 * h + 1, :], bank4.sl(256, 128),
                        og_fm[:, 2 * h + 1, c * C:(c + 1) * C], ALU.mult))
                    bank7 = pb[7]
                    b.wait("t", tkh)
                    mm(b, bank7, bank7.sl(par * 128, 128), khb[par][:, :],
                       iI128b[:, :], start=True, stop=True, seal=True)
                    tkt = pbread(b, bank7, "s", lambda E, bank7=bank7, par=par: E.copy(
                        khT[par][:, :], bank7.sl(par * 128, 128)))
                    b.wait("t", tkt)
                    mm(b, bank7, bank7.sl(256, 256), khT[par][:, :],
                       v_tm[:, c, h * DV:(h + 1) * DV], start=True, stop=True, seal=True)
                    b.wait("v", tEB)
                    expB_rd[par] = pbread(b, bank7, "v", lambda E, bank7=bank7, par=par, h=h: E.scalar_tensor_tensor(
                        S_state[:, h * DV:(h + 1) * DV], S_state[:, h * DV:(h + 1) * DV],
                        expB_sb[par][:, :], bank7.sl(256, 256), ALU.mult, ALU.add))
                tog_done = b.tok("v")
                b.wait("t", tog_done)
                b.wait("t", tk_wo)
                for half in range(2):
                    bank = pb[half]
                    for kk in range(16):
                        mm(b, bank, bank.sl(0, 512), outog[:, kk, :],
                           Wo_sb[:, kk, half * 512:(half + 1) * 512],
                           start=(kk == 0), stop=(kk == 15), seal=(kk == 15))
                    pbread(b, bank, "s", lambda E, bank=bank, half=half: E.copy(
                        y_sb[:, half * 512:(half + 1) * 512], bank.t[:, 0:512]))
                tk_y = b.tok("s")
                if cfg.debug and ci == 0:
                    b.wait("y", tk_y)
                    tdm = b.dma("y", dbg["dbg_mix"][:, :], y_sb[:, :], dkey())
                    for _e in ("t", "v", "s"):
                        b.wait(_e, tdm)
                b.wait("v", tk_y)
                y3 = y_sb.rearrange("p (a n) -> p a n", a=1)
                emit_ln1(y3, 1, st_off=4)
                tyn = emit_norm_sub(y3, 0, yn_sb, st_off=4)
                b.wait("v", tyn)
                tx1 = b.opi("v", lambda E, c=c: E.tensor_tensor(
                    x_tm[:, c, :], x_tm[:, c, :], yn_sb[:, :], ALU.add))
                b.wait("y", tx1)
                x1_store[0] = b.dma(
                    "y", x1_sp.rearrange("(a p) n -> p a n", p=128)[:, ci, :],
                    x_tm[:, c, :], "st1")
        own_es.close()

    def emit_ffn():
        ffn_es = ExitStack()
        fsb = lambda name, shape, dt: ffn_es.enter_context(
            nc.sbuf_tensor(name, list(shape), dt))
        x1ln_fm = fsb("x1ln_fm", [128, 8, N_OWN], BF16)
        acc = fsb("acc", [128, n_tiles, D], F32)
        cw_bf = fsb("cw_bf", [128, n_tiles * NE], BF16)
        cwT64 = fsb("cwT64", [64, N_OWN], BF16)
        iota64 = fsb("iota64", [64, 1], F32)
        ones64 = fsb("ones64", [64, 128], BF16)
        selg = fsb("selg", [64, 4 * 128], BF16)
        selm = fsb("selm", [64, 1], F32)
        cwb_sb = fsb("cwb_sb", [128, 2, 384], BF16)
        h_sc = fsb("h_sc", [128, 2, 2, 384], BF16)
        scale_tok = [None, None]
        x1a = fsb("x1a", [128, D], F32)
        x1b = fsb("x1b", [128, D], F32)
        nmx = fsb("nmx", [128, n_tiles], F32)
        rec_sb = fsb("rec_sb", [128, n_tiles], F32)
        rw_sb = fsb("rw_sb", [128, 8, NE], BF16)
        rb_sb = fsb("rb_sb", [1, NE], BF16)
        b1T_sb = fsb("b1T_sb", [128, 2, NE], F32)
        sb1T_sb = fsb("sb1T_sb", [128, 16, 2], F32)
        b2_sb = fsb("b2_sb", [64, D], BF16)
        sb2_sb = fsb("sb2_sb", [2, D], BF16)
        f1_es = ExitStack()
        probs = f1_es.enter_context(nc.sbuf_tensor("probs", [128, n_tiles, NE], F32))
        pw = f1_es.enter_context(nc.sbuf_tensor("pw", [128, n_tiles, NE], F32))

        b.dma("y", iota64[:, :], c_iota64[:, :], "fc")
        b.dma("y", ones64[:, :], c_ones64b[:, :], "fc")
        b.dma("g", rw_sb[:, :, :], rw.rearrange("(a p) n -> p a n", p=128), "gc")
        b.dma("g", rb_sb[:, :], rb[:, :], "gc")
        b.dma("y", b1T_sb[:, :, :], b1T.rearrange("(a p) n -> p a n", p=128), "fc")
        b.dma("y", sb1T_sb[:, :, :], sb1T.rearrange("(a p) n -> p a n", p=128), "fc")
        b.dma("g", b2_sb[:, :], b2[:, :], "gc")
        tk_fc = b.dma("g", sb2_sb[:, :], sb2[:, :], "gc")
        tk_fy = ("fc", b.sem["fc"][1])
        tkacc = b.opi("v", lambda E: E.memset(acc[:, :, :], 0.0))

        x1bufs = [x1a, x1b]
        x1_toks = [None, None]
        if "st1" in b.sem:
            b.wait("y", ("st1", b.sem["st1"][1]))

        for ti in range(n_tiles):
            xb = x1bufs[ti % 2]
            if x1_toks[ti % 2] is not None:
                b.wait("y", x1_toks[ti % 2][0])
                b.wait("y", x1_toks[ti % 2][1])
            tkx = b.dma("y", xb[:, :],
                        x1_sp.rearrange("(a p) n -> p a n", p=128)[:, ti, :],
                        "fx%d" % (ti % 2))
            b.wait("v", tkx)
            b.wait("s", tkx)
            xb3 = xb.rearrange("p (a n) -> p a n", a=1)
            emit_ln1(xb3, 1, st_off=ti % 2)
            tkn = emit_norm_sub(xb3, 0, xlnbuf[ti % 2], st_off=ti % 2)
            x1_toks[ti % 2] = (b.tok("v"), b.tok("s"))
            emit_transpose_to_fm(xlnbuf[ti % 2], x1ln_fm, tkn, ti * 128)
            tk_fmd = b.tok("s")
            b.wait("t", tk_fmd)
            b.wait("t", tk_fc)
            b.wait("t", tk_fy)
            bank = pb[6]
            for kc in range(8):
                mm(b, bank, bank.sl(0, NE), x1ln_fm[:, kc, ti * 128:(ti + 1) * 128],
                   rw_sb[:, kc, :], start=(kc == 0), stop=False)
            mm(b, bank, bank.sl(0, NE), iones1b[:, :], rb_sb[:, :],
               start=False, stop=True, seal=True)
            tnm = pbread(b, bank, "v", lambda E, bank=bank, ti=ti: E.reduce_max(
                nmx[:, ti:ti + 1], bank.sl(0, NE), axis=AX.X, negate=True))
            b.wait("s", tnm)
            tpr = pbread(b, bank, "s", lambda E, bank=bank, ti=ti: E.activation(
                probs[:, ti, :], bank.sl(0, NE), AF.Exp,
                bias=nmx[:, ti:ti + 1], scale=1.0))
            b.wait("v", tpr)
            b.opi("v", lambda E, ti=ti: E.tensor_copy(pw[:, ti, :], probs[:, ti, :]))

        if cfg.debug:
            tv = b.tok("s")
            b.wait("y", tv)
            tdp_ = b.dma("y", dbg["dbg_probs"][:, :], probs[:, 0, :], dkey())
            for _e in ("t", "v", "s"):
                b.wait(_e, tdp_)

        def vbatch(mkfn):
            for ti in range(n_tiles):
                fn = mkfn(ti)
                if ti == n_tiles - 1:
                    b.opi("v", fn)
                else:
                    b.op("v", fn)
            b.vbounce()
        for r in range(TOPK):
            vbatch(lambda ti: lambda E, ti=ti: E.reduce_max(
                nmx[:, ti:ti + 1], pw[:, ti, :], axis=AX.X))
            if r < TOPK - 1:
                vbatch(lambda ti: lambda E, ti=ti: E.scalar_tensor_tensor(
                    pw[:, ti, :], pw[:, ti, :], nmx[:, ti:ti + 1],
                    pw[:, ti, :], ALU.is_lt, ALU.mult))
        vbatch(lambda ti: lambda E, ti=ti: E.scalar_tensor_tensor(
            pw[:, ti, :], probs[:, ti, :], nmx[:, ti:ti + 1],
            probs[:, ti, :], ALU.is_ge, ALU.mult))
        vbatch(lambda ti: lambda E, ti=ti: E.reduce_sum(
            nmx[:, ti:ti + 1], pw[:, ti, :], axis=AX.X))
        b.opi("v", lambda E: E.reciprocal(rec_sb[:, 0:n_tiles], nmx[:, 0:n_tiles]))
        b.vbounce()
        tlast = None
        for ti in range(n_tiles):
            fn = lambda E, ti=ti: E.tensor_scalar(
                pw[:, ti, :], pw[:, ti, :], rec_sb[:, ti:ti + 1], None, ALU.mult)
            tlast = b.opi("v", fn) if ti == n_tiles - 1 else b.op("v", fn)
        b.wait("s", tlast)
        tcw = b.opi("s", lambda E: E.copy(
            cw_bf[:, :], pw[:, :, :].rearrange("p a n -> p (a n)")))
        if cfg.debug:
            b.wait("y", tcw)
            tdc = b.dma("y", dbg["dbg_cw"][:, :], pw[:, 0, :], dkey())
            for _e in ("t", "v", "s"):
                b.wait(_e, tdc)
        b.wait("t", tcw)
        for ti in range(n_tiles):
            bank = pb[6]
            mm(b, bank, bank.sl(0, 128, 0, 64), cw_bf[:, ti * NE:(ti + 1) * NE],
               iI128b[:, :], start=True, stop=True, seal=True)
            pbread(b, bank, "s", lambda E, bank=bank, ti=ti: E.copy(
                cwT64[:, ti * 128:(ti + 1) * 128], bank.sl(0, 128, 0, 64)))
        tk_cwT = b.tok("s")

        if cfg.stop_after == "router":
            ffn_es.close()
            return
        b.barrier(("t", "v", "s", "y", "g"))
        f1_es.close()
        w1g = [fsb(f"w1g{i}", [128, 8, 4 * EH], BF16) for i in range(2)]
        w2g = [fsb(f"w2g{i}", [128, 2 * 4, D], BF16) for i in range(2)]

        groups = []
        for gi in range(16):
            groups.append([("r", gi * 4 + j) for j in range(4)])
        for gi in range(4):
            groups.append([("s", gi * 4 + j) for j in range(4)])

        def w_dma(slot, grp):
            toks = []
            for j, (kind, e) in enumerate(grp):
                sk = "gw%d" % slot
                if kind == "r":
                    t1 = b.dma("g", w1g[slot][:, :, j * EH:(j + 1) * EH],
                               w1[e].rearrange("(a p) n -> p a n", p=128), sk)
                    t2 = b.dma("g", w2g[slot][:, j * 2:(j + 1) * 2, :],
                               w2[e].rearrange("(a p) n -> p a n", p=128), sk)
                else:
                    si, ci_ = e // 8, e % 8
                    t1 = b.dma("g", w1g[slot][:, :, j * EH:(j + 1) * EH],
                               sw1[si][:, ci_ * EH:(ci_ + 1) * EH]
                               .rearrange("(a p) n -> p a n", p=128), sk)
                    t2 = b.dma("g", w2g[slot][:, j * 2:(j + 1) * 2, :],
                               sw2[si][ci_ * EH:(ci_ + 1) * EH, :]
                               .rearrange("(a p) n -> p a n", p=128), sk)
                toks.append((t1, t2))
            return toks[-1][1]

        wtoks = [None, None]
        wtoks[0] = w_dma(0, groups[0])
        if len(groups) > 1:
            wtoks[1] = w_dma(1, groups[1])

        for gidx, grp in enumerate(groups):
            slot = gidx % 2
            first_grp = (gidx == 0)
            ts0 = 0
            for tsi, tscnt in enumerate(ffn_ts):
                ntq = tscnt // 128
                if first_grp:
                    b.wait("t", tk_cwT)
                    b.wait("t", tk_fc)
                    for tq in range(ntq):
                        g128 = ts0 // 128 + tq
                        for half in range(2):
                            bank = pb[tq * 2 + half]
                            mm(b, bank, bank.sl(0, 512),
                               cwT64[:, g128 * 128:(g128 + 1) * 128],
                               b2_sb[:, half * 512:(half + 1) * 512],
                               start=True, stop=False)
                            mm(b, bank, bank.sl(0, 512), iones2b[:, :],
                               sb2_sb[:, half * 512:(half + 1) * 512],
                               start=False, stop=False)
                if tsi == 0 and grp[0][0] == "r":
                    b.wait("v", b.tok("t"))
                    for j, (kind, e) in enumerate(grp):
                        b.opi("v", lambda E, e=e: E.tensor_scalar(
                            selm[:, :], iota64[:, :], float(e), None, ALU.is_equal))
                        b.opi("v", lambda E, j=j: E.tensor_scalar(
                            selg[:, j * 128:(j + 1) * 128], ones64[:, :],
                            selm[:, :], None, ALU.mult))
                    tk_sel = b.tok("v")
                    b.wait("t", tk_sel)
                def front(j, kind, e):
                    par = j % 2
                    bankh = pb[6 + par]
                    b.wait("t", wtoks[slot])
                    if kind == "r":
                        b.wait("t", tk_cwT)
                        mm(b, bankh, bankh.sl(0, tscnt),
                           selg[:, j * 128:(j + 1) * 128],
                           cwT64[:, ts0:ts0 + tscnt], start=True, stop=True, seal=True)
                        b.wait("s", scale_tok[par])
                        pbread(b, bankh, "s",
                               lambda E, bankh=bankh, par=par, tscnt=tscnt: E.copy(
                                   cwb_sb[:, par, 0:tscnt], bankh.sl(0, tscnt)))
                    else:
                        b.wait("s", scale_tok[par])
                    for hc in range(2):
                        for kc in range(8):
                            mm(b, bankh, bankh.sl(0, tscnt),
                               w1g[slot][:, kc, j * EH + hc * 128:j * EH + (hc + 1) * 128],
                               x1ln_fm[:, kc, ts0:ts0 + tscnt],
                               start=(kc == 0), stop=(kc == 7), seal=(kc == 7))
                        if kind == "r":
                            bias_ap = b1T_sb[:, hc, e:e + 1]
                        else:
                            si, ci_ = e // 8, e % 8
                            bias_ap = sb1T_sb[:, ci_ * 2 + hc, si:si + 1]
                        dst = h_sc
                        pbread(b, bankh, "s",
                               lambda E, bankh=bankh, par=par, hc=hc, bias_ap=bias_ap,
                               dst=dst, tscnt=tscnt: E.activation(
                                   dst[:, par, hc, 0:tscnt], bankh.sl(0, tscnt),
                                   AF.Silu, bias=bias_ap, scale=1.0))
                    tk_h = b.tok("s")
                    if kind == "r":
                        b.wait("v", tk_h)
                        for hc in range(2):
                            b.op("v", lambda E, par=par, hc=hc, tscnt=tscnt: E.tensor_tensor(
                                h_sc[:, par, hc, 0:tscnt], h_sc[:, par, hc, 0:tscnt],
                                cwb_sb[:, par, 0:tscnt], ALU.mult))
                        scale_tok[par] = b.tok("v")
                    else:
                        scale_tok[par] = tk_h
                    return scale_tok[par]

                def back(j, kind, e, rdy):
                    par = j % 2
                    b.wait("t", rdy)
                    for tq in range(ntq):
                        for hc in range(2):
                            for half in range(2):
                                bank = pb[tq * 2 + half]
                                first_w = (j == 0 and hc == 0 and not first_grp)
                                last = (j == len(grp) - 1 and hc == 1)
                                mm(b, bank, bank.sl(0, 512),
                                   h_sc[:, par, hc, tq * 128:(tq + 1) * 128],
                                   w2g[slot][:, j * 2 + hc, half * 512:(half + 1) * 512],
                                   start=first_w, stop=last, seal=last)

                pend = None
                for j, (kind, e) in enumerate(grp):
                    rdy = front(j, kind, e)
                    if pend is not None:
                        back(*pend)
                    pend = (j, kind, e, rdy)
                back(*pend)
                for tq in range(ntq):
                    g128 = ts0 // 128 + tq
                    for half in range(2):
                        bank = pb[tq * 2 + half]
                        pbread(b, bank, "v",
                               lambda E, bank=bank, g128=g128, half=half: E.tensor_tensor(
                                   acc[:, g128, half * 512:(half + 1) * 512],
                                   acc[:, g128, half * 512:(half + 1) * 512],
                                   bank.t[:, 0:512], ALU.add))
                ts0 += tscnt
            if gidx + 2 < len(groups):
                b.wait("g", b.tok("t"))
                wtoks[slot] = w_dma(slot, groups[gidx + 2])
        tk_accv = b.opi("v", lambda E: E.tensor_copy(st_s1[:, 0:1], st_s1[:, 0:1]))
        if cfg.debug:
            b.wait("y", tk_accv)
            tdac = b.dma("y", dbg["dbg_acc"][:, :], acc[:, 0, :], dkey())
            for _e in ("t", "v", "s"):
                b.wait(_e, tdac)

        ydma = None
        for ti in range(n_tiles):
            xb = x1bufs[ti % 2]
            if ydma is not None:
                b.wait("v", ydma)
            if x1_toks[ti % 2] is not None:
                b.wait("y", x1_toks[ti % 2][0])
                b.wait("y", x1_toks[ti % 2][1])
            tkx = b.dma("y", xb[:, :],
                        x1_sp.rearrange("(a p) n -> p a n", p=128)[:, ti, :],
                        "fx%d" % (ti % 2))
            a3 = acc[:, ti, :].rearrange("p (a n) -> p a n", a=1)
            b.wait("v", tkacc)
            b.wait("s", tk_accv)
            emit_ln1(a3, 1, st_off=ti % 2)
            b.wait("s", ydma)
            tyn = emit_norm_sub(a3, 0, y_sb, st_off=ti % 2)
            b.wait("v", tyn)
            b.wait("v", tkx)
            tfin = b.opi("v", lambda E, xb=xb: E.tensor_tensor(
                y_sb[:, :], y_sb[:, :], xb[:, :], ALU.add))
            x1_toks[ti % 2] = (tfin, tfin)
            b.wait("y", tfin)
            ydma = b.dma("y", y_out.rearrange("(a p) n -> p a n", p=128)[:, ti, :],
                         y_sb[:, :], "sty")
        ffn_es.close()

    emit_prefix()
    pre_es.close()
    if cfg.stop_after != "prefix":
        b.barrier()
        emit_own()
    mix_es.close()
    if cfg.stop_after not in ("prefix", "mixer"):
        b.barrier()
        emit_ffn()

    b.barrier()

    with nc.Block() as block:
        @block.tensor
        def _(t):
            for fn in b.ops["t"]:
                fn(t)

        @block.vector
        def _(v):
            for fn in b.ops["v"]:
                fn(v)

        @block.scalar
        def _(s):
            for fn in b.ops["s"]:
                fn(s)

        @block.sync
        def _(y):
            for fn in b.ops["y"]:
                fn(y)

        @block.gpsimd
        def _(g):
            for fn in b.ops["g"]:
                fn(g)

    return nc, es, b



def make_consts():
    sel8 = np.zeros((8, 8 * 128), np.float32)
    for h in range(8):
        sel8[h, h * 128:(h + 1) * 128] = 1.0
    mask = np.where(np.arange(128)[:, None] <= np.arange(128)[None, :],
                    0.0, -1e30).astype(np.float32)
    return {
        "c_I128b": np.eye(128, dtype=np.float32).astype(BF),
        "c_I8f": np.eye(8, dtype=np.float32),
        "c_sel8f": sel8,
        "c_sel8b": sel8.astype(BF),
        "c_iota64": np.arange(64, dtype=np.float32).reshape(64, 1),
        "c_ones64b": np.ones((64, 128), np.float32).astype(BF),
        "c_mask": mask,
        "c_ones1b": np.ones((1, 128), np.float32).astype(BF),
        "c_ones2b": np.ones((2, 128), np.float32).astype(BF),
        "c_eps": np.full((128, 1), 1e-5, np.float32),
    }


def make_in_maps(inputs, cfg):
    x = np.ascontiguousarray(np.asarray(inputs["L0_attn_x"], dtype=np.float32))
    consts = make_consts()
    shared = {
        "Wq": np.ascontiguousarray(inputs["L0_attn_Wq"], dtype=np.float32),
        "Wk": np.ascontiguousarray(inputs["L0_attn_Wk"], dtype=np.float32),
        "Wv": np.ascontiguousarray(inputs["L0_attn_Wv"], dtype=np.float32),
        "Wo": np.ascontiguousarray(inputs["L0_attn_Wo"], dtype=np.float32),
        "Wg": np.ascontiguousarray(inputs["L0_attn_Wg"], dtype=np.float32),
        "bg": np.ascontiguousarray(
            np.asarray(inputs["L0_attn_bg"], dtype=np.float32).reshape(H, 1)),
        "Wog": np.ascontiguousarray(inputs["L0_attn_Wog"], dtype=np.float32),
        "rw": np.ascontiguousarray(inputs["L0_ffn_router_w"], dtype=np.float32),
        "rb": np.ascontiguousarray(
            np.asarray(inputs["L0_ffn_router_b"], dtype=np.float32).reshape(1, NE)),
        "w1": np.ascontiguousarray(inputs["L0_ffn_w1"], dtype=np.float32),
        "b1T": np.ascontiguousarray(
            np.asarray(inputs["L0_ffn_b1"], dtype=np.float32).T),
        "w2": np.ascontiguousarray(inputs["L0_ffn_w2"], dtype=np.float32),
        "b2": np.ascontiguousarray(inputs["L0_ffn_b2"], dtype=np.float32),
        "sw1": np.ascontiguousarray(inputs["L0_ffn_sw1"], dtype=np.float32),
        "sb1T": np.ascontiguousarray(
            np.asarray(inputs["L0_ffn_sb1"], dtype=np.float32).T),
        "sw2": np.ascontiguousarray(inputs["L0_ffn_sw2"], dtype=np.float32),
        "sb2": np.ascontiguousarray(inputs["L0_ffn_sb2"], dtype=np.float32),
    }
    shared.update(consts)
    in_maps = []
    for core in range(8):
        bi, r = core // 4, core % 4
        p = r * cfg.n_own
        xp = np.zeros((max(cfg.p_pre, 1), D), np.float32)
        take = min(p, cfg.p_pre)
        if take > 0:
            xp[cfg.p_pre - take:] = x[bi, p - take:p]
        m = dict(shared)
        m["x_pre"] = xp
        m["x_own"] = np.ascontiguousarray(x[bi, p:p + cfg.n_own])
        in_maps.append(m)
    return in_maps


_CFG = Cfg()
_CACHE = {}


def kernel(**inputs):
    cfg = _CFG
    key = (cfg.p_pre, cfg.n_own, cfg.stop_after, cfg.debug)
    if key not in _CACHE:
        _CACHE[key] = build_program(cfg)
    nc, _es = _CACHE[key]
    in_maps = make_in_maps(inputs, cfg)
    res = run_bass_kernel_spmd(nc, in_maps, core_ids=list(range(8)))
    B = np.asarray(inputs["L0_attn_x"]).shape[0]
    S = np.asarray(inputs["L0_attn_x"]).shape[1]
    y = np.empty((B, S, D), np.float32)
    for core in range(8):
        bi, r = core // 4, core % 4
        y[bi, r * cfg.n_own:(r + 1) * cfg.n_own] = res.results[core]["y"]
    return y


# version 06 (final)
# speedup vs baseline: 1.0471x; 1.0471x over previous
import numpy as np
import ml_dtypes
from contextlib import ExitStack

import concourse.bass as bass
import concourse.mybir as mybir
from concourse.bass_utils import run_bass_kernel_spmd

F32 = mybir.dt.float32
BF16 = mybir.dt.bfloat16
AF = mybir.ActivationFunctionType
ALU = mybir.AluOpType
AX = mybir.AxisListType
BF = ml_dtypes.bfloat16

H, DK, DV, D, HDV = 8, 128, 256, 1024, 2048
C = 128
TS = 512
NE, TOPK = 64, 8
EH = 256
SH = 2048
RSCALE = float(DK) ** -0.5


class Cfg:
    def __init__(self, p_pre=6144, n_own=2048, stop_after=None, debug=False,
                 micro=99):
        self.p_pre = p_pre
        self.n_own = n_own
        self.stop_after = stop_after
        self.debug = debug
        self.micro = micro


def _ffn_tsups(n_own):
    out, rem = [], n_own
    while rem >= 384:
        out.append(384)
        rem -= 384
    if rem:
        out.append(rem)
    return out


class Bld:
    ENG = ("t", "v", "s", "y", "g")

    def __init__(self, nc, es, dma_plan=None):
        self.nc = nc
        self.es = es
        self.dma_plan = dma_plan
        self.dma_calls = {}
        self.ops = {e: [] for e in self.ENG}
        self.sem = {}
        for e in ("t", "v", "s"):
            self._mksem(e)
        self.waited = {e: {} for e in self.ENG}
        self.dma_keys = set()

    def _mksem(self, name):
        h = self.es.enter_context(self.nc.semaphore("sem_" + name))
        self.sem[name] = [h, 0]
        return self.sem[name]

    AUTO_SER = ("v", "s")

    def emit(self, e, fn):
        self.ops[e].append(fn)

    def op(self, e, fn):
        if e in self.AUTO_SER:
            return self.opi(e, fn)
        self.emit(e, fn)
        return None

    def opi(self, e, fn):
        h, n = self.sem[e]
        if e in self.AUTO_SER and n > self.waited[e].get(e, 0):
            self.waited[e][e] = n
            self.emit(e, lambda E, h=h, c=n: E.wait_ge(h, c))
        self.sem[e][1] += 1
        self.emit(e, lambda E, fn=fn, h=h: fn(E).then_inc(h, 1))
        return (e, self.sem[e][1])

    def tok(self, e):
        return (e, self.sem[e][1])

    def wait(self, e, tok):
        if tok is None:
            return
        pe, cnt = tok
        if pe == e and e not in self.AUTO_SER:
            return
        if self.waited[e].get(pe, 0) >= cnt:
            return
        self.waited[e][pe] = cnt
        h = self.sem[pe][0]
        self.emit(e, lambda E, h=h, c=cnt: E.wait_ge(h, c))

    wait_force = wait

    def dma(self, qe, out, in_, skey):
        if skey not in self.sem:
            self._mksem(skey)
            self.dma_keys.add(skey)
        idx = self.dma_calls.get(skey, 0)
        self.dma_calls[skey] = idx + 1
        amt = 16
        if self.dma_plan is not None:
            amt = self.dma_plan[skey][idx]
        self.sem[skey][1] += amt
        h = self.sem[skey][0]
        self.emit(qe, lambda E, o=out, i=in_, h=h: E.dma_start(
            out=o, in_=i).then_inc(h, 16))
        return (skey, self.sem[skey][1])

    def vbounce(self, vtok=None):
        if vtok is None:
            vtok = self.tok("v")
        self.wait("s", vtok)
        ap = self._bounce_ap
        ts = self.opi("s", lambda E, ap=ap: E.copy(ap[0:1, 0:1], ap[0:1, 1:2]))
        self.wait("v", ts)
        return ts

    def barrier(self, engines=None):
        engines = engines or self.ENG
        toks = [("t", self.sem["t"][1]), ("v", self.sem["v"][1]),
                ("s", self.sem["s"][1])]
        toks += [(k, self.sem[k][1]) for k in sorted(self.dma_keys)]
        for e in engines:
            for tok in toks:
                self.wait(e, tok)


class PB:
    def __init__(self, t):
        self.t = t
        self.wtok = None
        self.rtoks = []
        self.open = False
        self.unsealed = False

    def sl(self, off, n, poff=0, pn=128):
        return self.t[poff:poff + pn, off:off + n]


def mm(b, pb, out, lhsT, rhs, start, stop, seal=False):
    if start:
        assert not pb.open, "interleaved accumulation groups in one PSUM bank"
        for tok in pb.rtoks:
            b.wait("t", tok)
        pb.rtoks = []
        pb.open = True
    if stop:
        pb.open = False
    pb.unsealed = True
    fn = lambda E, o=out, l=lhsT, r=rhs, st=start, sp=stop: E.matmul(
        o, l, r, start=st, stop=sp)
    if seal:
        pb.wtok = b.opi("t", fn)
        pb.unsealed = False
    else:
        b.op("t", fn)


def pbread(b, pb, e, fn):
    assert not pb.unsealed, "reading PSUM bank with unsealed writes"
    b.wait(e, pb.wtok)
    tok = b.opi(e, fn)
    pb.rtoks.append(tok)
    return tok



def _extract_dma_plan(nc):
    from concourse.bass_interp import CoreSim
    sim = CoreSim(nc, no_exec=True)
    st = sim._sim_state
    plan = {}
    for name in nc.inst_names():
        raw = nc.find_inst_by_name(name)
        if raw is None or type(raw).__name__ != "InstDMACopy":
            continue
        si = raw.sync_info
        if not si or not si.on_update:
            continue
        u = si.on_update[0]
        if not u.ant_name.startswith("sem_"):
            continue
        amt = st.sem_dry_run_update(u.id, u, raw) - st.sem_value(u.id)
        plan.setdefault(u.ant_name[4:], []).append(int(amt))
    return plan


def build_program(cfg):
    nc1, es1, b1 = _build(cfg, None)
    plan = _extract_dma_plan(nc1)
    for k, n in b1.dma_calls.items():
        got = len(plan.get(k, []))
        assert got == n, f"dma plan mismatch for {k}: {got} updates vs {n} calls"
    es1.close()
    nc2, es2, _ = _build(cfg, plan)
    return nc2, es2


def _build(cfg, dma_plan):
    nc = bass.Bass("TRN2", target_bir_lowering=False)
    es = ExitStack()

    P_PRE, N_OWN = cfg.p_pre, cfg.n_own
    n_pre_st = P_PRE // TS
    n_own_st = N_OWN // TS
    n_own_ch = N_OWN // C
    ffn_ts = _ffn_tsups(N_OWN)
    n_tiles = N_OWN // C

    dr = {}
    def din(name, shape, dt=F32):
        dr[name] = nc.dram_tensor(name, list(shape), dt, kind="ExternalInput").ap()
        return dr[name]

    x_pre = din("x_pre", [max(P_PRE, 1), D])
    x_own = din("x_own", [N_OWN, D])
    Wq = din("Wq", [D, H * DK]); Wk = din("Wk", [D, H * DK])
    Wv = din("Wv", [D, H * DV]); Wo = din("Wo", [HDV, D])
    Wg = din("Wg", [D, H]); bg = din("bg", [H, 1]); Wog = din("Wog", [D, HDV])
    rw = din("rw", [D, NE]); rb = din("rb", [1, NE])
    w1 = din("w1", [NE, D, EH]); b1T = din("b1T", [EH, NE])
    w2 = din("w2", [NE, EH, D]); b2 = din("b2", [NE, D])
    sw1 = din("sw1", [2, D, SH]); sb1T = din("sb1T", [SH, 2])
    sw2 = din("sw2", [2, SH, D]); sb2 = din("sb2", [2, D])
    c_I128b = din("c_I128b", [128, 128], BF16)
    c_I8f = din("c_I8f", [8, 8])
    c_sel8f = din("c_sel8f", [8, 8 * 128])
    c_sel8b = din("c_sel8b", [8, 8 * 128], BF16)
    c_iota64 = din("c_iota64", [64, 1])
    c_ones64b = din("c_ones64b", [64, 128], BF16)
    c_mask = din("c_mask", [128, 128])
    c_ones1b = din("c_ones1b", [1, 128], BF16)
    c_ones2b = din("c_ones2b", [2, 128], BF16)
    c_eps = din("c_eps", [128, 1])

    x1_sp = nc.dram_tensor("x1_sp", [N_OWN, D], F32, kind="Internal").ap()
    y_out = nc.dram_tensor("y", [N_OWN, D], F32, kind="ExternalOutput").ap()
    dbg = {}
    if cfg.debug:
        for nm, shp in [("dbg_S", [128, H * DV]), ("dbg_xln", [128, D]),
                        ("dbg_q", [128, TS]), ("dbg_k", [128, TS]),
                        ("dbg_v", [128, HDV]), ("dbg_og", [128, TS]),
                        ("dbg_la", [8, TS]), ("dbg_b", [8, TS]),
                        ("dbg_mix", [128, D]), ("dbg_probs", [128, NE]),
                        ("dbg_cw", [128, NE]), ("dbg_acc", [128, D]),
                        ("dbg_h", [128, EH]),
                        ("dbg_xa", [128, D]), ("dbg_fm", [128, TS]),
                        ("dbg_alpha", [8, TS]), ("dbg_stats", [128, 8])]:
            dbg[nm] = nc.dram_tensor(nm, shp, F32, kind="ExternalOutput").ap()

    b = Bld(nc, es, dma_plan)
    _dbgn = [0]

    def dkey():
        _dbgn[0] += 1
        return "dbg%d" % _dbgn[0]

    sb = lambda name, shape, dt: es.enter_context(nc.sbuf_tensor(name, list(shape), dt))

    pb = [PB(es.enter_context(nc.psum_tensor(f"pb{i}", [128, 512], F32)))
          for i in range(8)]
    iI128b = sb("iI128b", [128, 128], BF16)
    iones1b = sb("iones1b", [1, 128], BF16)
    iones2b = sb("iones2b", [2, 128], BF16)
    CCm = TS // C
    expB_rd = [None, None]
    last_stt = [None]
    y_sb = sb("y_sb", [128, D], F32)
    sqbuf = [sb("sq0", [128, D], F32), sb("sq1", [128, D], F32)]
    sq_rd = [None, None]
    st_s1 = sb("st_s1", [128, 8], F32)
    st_sq = sb("st_sq", [128, 8], F32)
    st_mean = sb("st_mean", [128, 8], F32)
    st_msq = sb("st_msq", [128, 8], F32)
    st_var = sb("st_var", [128, 8], F32)
    st_rstd = sb("st_rstd", [128, 8], F32)
    st_mr = sb("st_mr", [128, 8], F32)
    ieps = sb("ieps", [128, 1], F32)
    yn_sb = sb("yn_sb", [128, D], F32)
    xlnA = sb("xlnA", [128, D], BF16)
    xlnB = sb("xlnB", [128, D], BF16)
    bounce = sb("bounce", [1, 2], F32)
    b._bounce_ap = bounce

    b.dma("y", iI128b[:, :], c_I128b[:, :], "c0")
    b.dma("y", iones1b[:, :], c_ones1b[:, :], "c0")
    b.dma("y", iones2b[:, :], c_ones2b[:, :], "c0")
    tk_eps = b.dma("y", ieps[:, :], c_eps[:, :], "c0")
    for _t in (st_s1, st_sq, st_mean, st_msq, st_var, st_rstd, st_mr, bounce):
        b.opi("v", lambda E, _t=_t: E.memset(_t[:, :], 0.0))

    mix_es = ExitStack()
    msb = lambda name, shape, dt: mix_es.enter_context(nc.sbuf_tensor(name, list(shape), dt))
    Wv_sb = msb("Wv_sb", [128, 8, H * DV], BF16)
    Wg_sb = msb("Wg_sb", [128, 8, H], BF16)
    x_tm = msb("x_tm", [128, CCm, D], F32)
    iI8f = msb("iI8f", [8, 8], F32)
    isel8f = msb("isel8f", [8, 8 * 128], F32)
    isel8b = msb("isel8b", [8, 8 * 128], BF16)
    imask = msb("imask", [128, 128], F32)
    S_state = msb("S_state", [128, H * DV], F32)
    bg_sb = msb("bg_sb", [H, 1], F32)
    bg16 = msb("bg16", [H, 1], F32)
    zeros8 = msb("zeros8", [8, C], F32)
    alpha = msb("alpha", [8, TS], F32)
    la = msb("la", [8, TS], F32)
    bsc = msb("bsc", [8, TS], F32)
    ts2 = msb("ts2", [8, TS], F32)
    Rxb = msb("Rxb", [8, TS], BF16)
    Rxm = msb("Rxm", [8, TS], BF16)
    btm = msb("btm", [128, CCm * 8], F32)
    Dpre = [msb(f"Dpre{p}", [128, 128], F32) for p in range(2)]
    Dt = [msb(f"Dt{p}", [128, 128], BF16) for p in range(2)]
    smb = [msb(f"smb{p}", [128, 128], BF16) for p in range(2)]
    qtb = [msb(f"qtb{p}", [128, 128], BF16) for p in range(2)]
    khb = [msb(f"khb{p}", [128, 128], BF16) for p in range(2)]
    khT = [msb(f"khT{p}", [128, 128], BF16) for p in range(2)]
    Sbf = [msb(f"Sbf{p}", [128, DV], BF16) for p in range(2)]
    expB_sb = msb("expB_sb", [128, H], F32)
    outog = msb("outog", [128, 16, 128], BF16)
    xln_fm = msb("xln_fm", [128, 8, TS], BF16)
    k_fm = msb("k_fm", [128, H, TS], BF16)
    v_tm = msb("v_tm", [128, CCm, H * DV], BF16)

    b.dma("y", iI8f[:, :], c_I8f[:, :], "c0")
    b.dma("y", isel8f[:, :], c_sel8f[:, :], "c0")
    b.dma("y", isel8b[:, :], c_sel8b[:, :], "c0")
    b.dma("y", imask[:, :], c_mask[:, :], "c0")
    tk_bg = b.dma("y", bg_sb[:, :], bg[:, :], "c0")
    tk_consts = tk_bg
    tk_eps = tk_bg
    b.wait("s", tk_bg)
    tk_bg16 = b.opi("s", lambda E: E.mul(bg16[:, :], bg_sb[:, :], 1.0 / 16.0))
    b.opi("v", lambda E: E.memset(zeros8[:, :], 0.0))
    tk_S0 = b.opi("v", lambda E: E.memset(S_state[:, :], 0.0))
    tk_wv = b.dma("g", Wv_sb[:, :, :], Wv.rearrange("(a p) n -> p a n", p=128), "wv")
    tk_wg = b.dma("g", Wg_sb[:, :, :], Wg.rearrange("(a p) n -> p a n", p=128), "wv")

    pre_es = ExitStack()
    Wk_sb = pre_es.enter_context(nc.sbuf_tensor("Wk_sb", [128, 8, H * DK], BF16))
    tk_wk = b.dma("g", Wk_sb[:, :, :], Wk.rearrange("(a p) n -> p a n", p=128), "wv")
    b.wait("v", tk_wk)
    tk_wks = b.opi("v", lambda E: E.tensor_scalar_mul(
        Wk_sb[:, :, :], Wk_sb[:, :, :], RSCALE))

    xlnbuf = [xlnA, xlnB]

    def emit_ln1(x_src_3d, cc, st_off=0):
        o = st_off
        for j in range(cc):
            sqb = sqbuf[j % 2]
            b.opi("v", lambda E, j=j: E.reduce_sum(
                st_s1[:, o + j:o + j + 1], x_src_3d[:, j, :], axis=AX.X))
            b.wait("s", sq_rd[j % 2])
            tsq = b.opi("s", lambda E, j=j, sqb=sqb: E.activation(
                sqb[:, :], x_src_3d[:, j, :], AF.Square))
            b.wait("v", tsq)
            sq_rd[j % 2] = b.opi("v", lambda E, j=j, sqb=sqb: E.reduce_sum(
                st_sq[:, o + j:o + j + 1], sqb[:, :], axis=AX.X))
        b.wait("s", b.tok("v"))
        b.opi("s", lambda E, o=o, cc=cc: E.mul(
            st_mean[:, o:o + cc], st_s1[:, o:o + cc], 1.0 / D))
        tmsq = b.opi("s", lambda E, o=o, cc=cc: E.activation(
            st_msq[:, o:o + cc], st_mean[:, o:o + cc], AF.Square))
        b.wait("v", tmsq)
        tvar = b.opi("v", lambda E, o=o, cc=cc: E.scalar_tensor_tensor(
            st_var[:, o:o + cc], st_sq[:, o:o + cc], 1.0 / D, st_msq[:, o:o + cc],
            ALU.mult, ALU.subtract))
        b.wait("s", tvar)
        b.wait("s", tk_eps)
        b.op("s", lambda E, o=o, cc=cc: E.activation(
            st_msq[:, o:o + cc], st_var[:, o:o + cc], AF.Ln,
            bias=ieps[:, :], scale=1.0))
        tkr = b.opi("s", lambda E, o=o, cc=cc: E.activation(
            st_rstd[:, o:o + cc], st_msq[:, o:o + cc], AF.Exp, bias=0.0, scale=-0.5))
        b.wait("v", tkr)
        tmr = b.opi("v", lambda E, o=o, cc=cc: E.scalar_tensor_tensor(
            st_mr[:, o:o + cc], st_mean[:, o:o + cc], -1.0, st_rstd[:, o:o + cc],
            ALU.mult, ALU.mult))
        b.wait("s", tmr)
        return tmr

    def emit_norm_sub(x_src_3d, j, dst, st_off=0):
        return b.opi("s", lambda E, j=j, dst=dst: E.activation(
            dst[:, :], x_src_3d[:, j, :], AF.Identity,
            bias=st_mr[:, st_off + j:st_off + j + 1],
            scale=st_rstd[:, st_off + j:st_off + j + 1]))

    def emit_transpose_to_fm(src_bf, dst_fm, tok_src, tslice0):
        b.wait("t", tok_src)
        b.wait("t", tk_consts)
        for half in range(2):
            bank = pb[2 + half]
            for q in range(4):
                dc = half * 4 + q
                mm(b, bank, bank.sl(q * 128, 128),
                   src_bf[:, dc * 128:(dc + 1) * 128], iI128b[:, :],
                   start=True, stop=True, seal=(q == 3))
            pbread(b, bank, "s", lambda E, bank=bank, half=half: E.copy(
                dst_fm[:, half * 4:half * 4 + 4, tslice0:tslice0 + 128],
                bank.t[:, 0:512].rearrange("p (a n) -> p a n", a=4)))
        return b.tok("s")

    wbuf_last = {}

    def proj_fm_stream(dst_fm, wbuf, wsrc_ap, n_slabs, Tcur, evict, skey="w?"):
        slab_tok = [None] * n_slabs
        use_tok = [None] * n_slabs
        prev = wbuf_last.get(id(wbuf))
        for i in range(min(2, n_slabs)):
            b.wait("g", prev)
            slab_tok[i] = b.dma("g", wbuf[:, i % 2, :, :],
                                wsrc_ap(i).rearrange("(a p) n -> p a n", p=128),
                                skey + str(i % 2))
        for i in range(n_slabs):
            b.wait("t", slab_tok[i])
            nparts = [(n0, min(512, Tcur - n0)) for n0 in range(0, Tcur, 512)]
            for pi, (n0, nn) in enumerate(nparts):
                bank = pb[2 + (i + pi) % 2]
                for kc in range(8):
                    mm(b, bank, bank.sl(0, nn),
                       wbuf[:, i % 2, kc, :], xln_fm[:, kc, n0:n0 + nn],
                       start=(kc == 0), stop=(kc == 7), seal=(kc == 7))
                evict(bank, i, n0, nn)
            use_tok[i] = b.tok("t")
            if i + 2 < n_slabs:
                b.wait("g", use_tok[i])
                slab_tok[i + 2] = b.dma("g", wbuf[:, (i + 2) % 2, :, :],
                                        wsrc_ap(i + 2).rearrange("(a p) n -> p a n", p=128),
                                        skey + str(i % 2))
        wbuf_last[id(wbuf)] = b.tok("t")

    def proj_fm_resident(dst_fm, Wres, n_slabs, Tcur, evict, scale_tok=None):
        for i in range(n_slabs):
            if scale_tok is not None:
                b.wait("t", scale_tok)
            nparts = [(n0, min(512, Tcur - n0)) for n0 in range(0, Tcur, 512)]
            for pi, (n0, nn) in enumerate(nparts):
                bank = pb[2 + (i + pi) % 2]
                for kc in range(8):
                    mm(b, bank, bank.sl(0, nn),
                       Wres[:, kc, i * 128:(i + 1) * 128], xln_fm[:, kc, n0:n0 + nn],
                       start=(kc == 0), stop=(kc == 7), seal=(kc == 7))
                evict(bank, i, n0, nn)

    def emit_vproj(cc, Tcur):
        for c in range(cc):
            for cb4 in range(4):
                bank = pb[2 + cb4 % 2]
                for kc in range(8):
                    mm(b, bank, bank.sl(0, 512),
                       xln_fm[:, kc, c * 128:(c + 1) * 128],
                       Wv_sb[:, kc, cb4 * 512:(cb4 + 1) * 512],
                       start=(kc == 0), stop=(kc == 7), seal=(kc == 7))
                pbread(b, bank, "v", lambda E, bank=bank, c=c, cb4=cb4: E.tensor_copy(
                    v_tm[:, c, cb4 * 512:(cb4 + 1) * 512], bank.t[:, 0:512]))
        return b.tok("v")

    def emit_gproj(cc, Tcur):
        nparts = [(n0, min(512, Tcur - n0)) for n0 in range(0, Tcur, 512)]
        for pi, (n0, nn) in enumerate(nparts):
            bank = pb[6]
            for kc in range(8):
                mm(b, bank, bank.sl(0, nn, 0, 8),
                   Wg_sb[:, kc, :], xln_fm[:, kc, n0:n0 + nn],
                   start=(kc == 0), stop=(kc == 7), seal=(kc == 7))
            b.wait("s", tk_bg16)
            pbread(b, bank, "s", lambda E, bank=bank, n0=n0, nn=nn: E.activation(
                alpha[:, n0:n0 + nn], bank.sl(0, nn, 0, 8), AF.Sigmoid,
                bias=bg16[:, :], scale=1.0 / 16.0))
        tka = b.tok("s")
        b.wait("v", tka)
        tkc = b.opi("v", lambda E, T=Tcur: E.tensor_scalar_max(
            alpha[:, 0:T], alpha[:, 0:T], 1e-38))
        b.wait("s", tkc)
        return b.opi("s", lambda E, T=Tcur: E.activation(
            la[:, 0:T], alpha[:, 0:T], AF.Ln, bias=0.0, scale=1.0))

    def emit_gateprep(cc, tk_la):
        b.wait("v", tk_la)
        micro = cfg.micro
        tscan = None
        for c in range(cc):
            fn = lambda E, c=c: E.tensor_tensor_scan(
                bsc[:, c * C:(c + 1) * C], la[:, c * C:(c + 1) * C],
                zeros8[:, :], 0.0, ALU.add, ALU.add)
            if c == cc - 1:
                tscan = b.opi("v", fn)
            else:
                b.op("v", fn)
        if micro < 5:
            return None, None
        b.wait("s", tscan)
        tRb = b.opi("s", lambda E, cc=cc: E.activation(
            Rxb[:, 0:cc * C], bsc[:, 0:cc * C], AF.Exp, bias=0.0, scale=1.0))
        if micro < 6:
            return None, None
        b.wait("t", tscan)
        bank = pb[2]
        for c in range(cc):
            mm(b, bank, bank.sl(c * 8, 8), bsc[:, c * C:(c + 1) * C], iI8f[:, :],
               start=True, stop=True, seal=(c == cc - 1))
        pbread(b, bank, "s", lambda E, bank=bank, cc=cc: E.copy(
            btm[:, 0:cc * 8], bank.t[:, 0:cc * 8]))
        if micro < 7:
            return None, None
        b.wait("v", tRb)
        tts2 = None
        for c in range(cc):
            fn = lambda E, c=c: E.tensor_scalar(
                ts2[:, c * C:(c + 1) * C], bsc[:, c * C:(c + 1) * C],
                bsc[:, c * C + C - 1:c * C + C], -1.0, ALU.subtract, ALU.mult)
            if c == cc - 1:
                tts2 = b.opi("v", fn)
            else:
                b.op("v", fn)
        b.wait("s", tts2)
        tkR = b.opi("s", lambda E, cc=cc: E.activation(
            Rxm[:, 0:cc * C], ts2[:, 0:cc * C], AF.Exp, bias=0.0, scale=1.0))
        return tkR, None

    khb_rd = [None, None]

    def emit_prefix_scan(cc, tkR, v_tok):
        bank6, bank7 = pb[6], pb[7]
        seq = [(c, h) for c in range(cc) for h in range(H)]
        st_f = {}

        def front(i):
            c, h = seq[i]
            par = i % 2
            _ = h
            b.wait("t", tkR)
            mm(b, bank6, bank6.sl(par * 256, 128), isel8b[:, h * 128:(h + 1) * 128],
               Rxb[:, c * C:(c + 1) * C], start=True, stop=True)
            mm(b, bank6, bank6.sl(par * 256 + 128, 128), isel8b[:, h * 128:(h + 1) * 128],
               Rxm[:, c * C:(c + 1) * C], start=True, stop=True, seal=True)
            b.wait("v", khb_rd[par])
            tkh = pbread(b, bank6, "v", lambda E, par=par, c=c, h=h: E.tensor_tensor(
                khb[par][:, :], k_fm[:, h, c * C:(c + 1) * C],
                bank6.sl(par * 256 + 128, 128), ALU.mult))
            tEB = pbread(b, bank6, "v", lambda E, par=par, h=h: E.tensor_copy(
                expB_sb[:, h:h + 1], bank6.sl(par * 256 + 127, 1)))
            st_f[i] = (tkh, tEB)

        def mid(i):
            par = i % 2
            tkh, tEB = st_f[i]
            b.wait("t", tkh)
            mm(b, bank7, bank7.sl(par * 128, 128), khb[par][:, :], iI128b[:, :],
               start=True, stop=True, seal=True)
            khb_rd[par] = ("t", b.sem["t"][1])
            b.wait("s", last_stt[0])
            tkt = pbread(b, bank7, "s", lambda E, par=par: E.copy(
                khT[par][:, :], bank7.sl(par * 128, 128)))
            st_f[i] = (tkt, tEB)

        def back(i):
            c, h = seq[i]
            par = i % 2
            tkt, tEB = st_f.pop(i)
            b.wait("t", tkt)
            if v_tok is not None:
                b.wait("t", v_tok)
            mm(b, bank7, bank7.sl(256, 256), khT[par][:, :],
               v_tm[:, c, h * DV:(h + 1) * DV], start=True, stop=True, seal=True)
            b.wait("v", tEB)
            tadd = pbread(b, bank7, "v", lambda E, par=par, h=h: E.scalar_tensor_tensor(
                S_state[:, h * DV:(h + 1) * DV], S_state[:, h * DV:(h + 1) * DV],
                expB_sb[:, h:h + 1], bank7.sl(256, 256), ALU.mult, ALU.add))
            expB_rd[par] = tadd
            last_stt[0] = tadd

        n = len(seq)
        for i in range(n + 2):
            if i < n:
                front(i)
            if 2 <= i:
                back(i - 2)
            if 1 <= i <= n:
                mid(i - 1)

    x_cons = [None]
    x_par = [0]

    def emit_prefix():
        for s in range(n_pre_st):
            if x_cons[0] is not None:
                b.wait("y", x_cons[0][0])
                b.wait("y", x_cons[0][1])
            tkx = b.dma("y", x_tm[:, :, :],
                        x_pre.rearrange("(a p) n -> p a n", p=128)[:, s * CCm:(s + 1) * CCm, :],
                        "x%d" % (x_par[0] % 2))
            x_par[0] += 1
            b.wait("v", tkx)
            b.wait("s", tkx)
            emit_ln1(x_tm, CCm)
            for j in range(CCm):
                tkn = emit_norm_sub(x_tm, j, xlnbuf[j % 2])
                emit_transpose_to_fm(xlnbuf[j % 2], xln_fm, tkn, j * 128)
            tk_fmdone = b.tok("s")
            x_cons[0] = (b.tok("v"), b.tok("s"))
            if cfg.micro < 2:
                continue
            b.wait("t", tk_fmdone)
            b.wait("t", tk_wk)
            b.wait("t", tk_wv)
            b.wait("t", tk_wg)
            def kev(bank, i, n0, nn):
                pbread(b, bank, "v", lambda E, bank=bank, i=i, n0=n0, nn=nn: E.tensor_copy(
                    k_fm[:, i, n0:n0 + nn], bank.sl(0, nn)))
            proj_fm_resident(k_fm, Wk_sb, H, TS, kev, scale_tok=tk_wks)
            v_tok = emit_vproj(CCm, TS)
            if cfg.micro < 3:
                x_cons[0] = (b.tok("v"), b.tok("s"))
                continue
            tk_la = emit_gproj(CCm, TS)
            x_cons[0] = (b.tok("v"), b.tok("s"))
            if cfg.micro < 4:
                continue
            tkR, _ = emit_gateprep(CCm, tk_la)
            x_cons[0] = (b.tok("v"), b.tok("s"))
            if cfg.micro < 8:
                continue
            emit_prefix_scan(CCm, tkR, v_tok)
        if cfg.debug:
            tkf = b.tok("v")
            b.wait("y", tkf)
            tdS = b.dma("y", dbg["dbg_S"][:, :], S_state[:, :], dkey())
            for _e in ("t", "v", "s"):
                b.wait(_e, tdS)

    def emit_own():
        own_es = ExitStack()
        osb = lambda name, shape, dt: own_es.enter_context(
            nc.sbuf_tensor(name, list(shape), dt))
        Wo_sb = osb("Wo_sb", [128, 16, D], BF16)
        q_fm = osb("q_fm", [128, H, TS], BF16)
        og_fm = osb("og_fm", [128, 16, TS], BF16)
        wq_buf = osb("wq_buf", [128, 2, 8, 128], BF16)
        wk_buf = osb("wk_buf", [128, 2, 8, 128], BF16)
        wog_buf = osb("wog_buf", [128, 2, 8, 128], BF16)

        tk_wo = b.dma("g", Wo_sb[:, :, :], Wo.rearrange("(a p) n -> p a n", p=128), "wo")
        x1_store = [None]
        dpre_rd = {0: None, 1: None}

        for s in range(n_own_st):
            if x_cons[0] is not None:
                b.wait("y", x_cons[0][0])
                b.wait("y", x_cons[0][1])
            if x1_store[0] is not None:
                b.wait_force("y", x1_store[0])
            tkx = b.dma("y", x_tm[:, :, :],
                        x_own.rearrange("(a p) n -> p a n", p=128)[:, s * CCm:(s + 1) * CCm, :],
                        "x%d" % (x_par[0] % 2))
            x_par[0] += 1
            b.wait("v", tkx)
            b.wait("s", tkx)
            emit_ln1(x_tm, CCm)
            for j in range(CCm):
                tkn = emit_norm_sub(x_tm, j, xlnbuf[j % 2])
                emit_transpose_to_fm(xlnbuf[j % 2], xln_fm, tkn, j * 128)
                if cfg.debug and s == 0 and j == 0:
                    tsn = b.tok("v")
                    b.wait("y", tsn)
                    tdd = b.dma("y", dbg["dbg_xln"][:, :], x_tm[:, 0, :], dkey())
                    for _e in ("t", "v", "s"):
                        b.wait(_e, tdd)
                    b.wait("g", tsn)
                    b.wait("g", b.tok("s"))
                    tdx = b.dma("g", dbg["dbg_xa"][:, :], xlnbuf[0][:, :], dkey())
                    for _e in ("t", "v", "s"):
                        b.wait(_e, tdx)

            tk_fmdone = b.tok("s")
            b.wait("t", tk_fmdone)
            def qev(bank, i, n0, nn):
                pbread(b, bank, "v", lambda E, bank=bank, i=i, n0=n0, nn=nn: E.tensor_copy(
                    q_fm[:, i, n0:n0 + nn], bank.sl(0, nn)))
            proj_fm_stream(q_fm, wq_buf, lambda i: Wq[:, i * 128:(i + 1) * 128],
                           H, TS, qev, skey="wq")
            def kev(bank, i, n0, nn):
                pbread(b, bank, "v", lambda E, bank=bank, i=i, n0=n0, nn=nn: E.tensor_scalar(
                    k_fm[:, i, n0:n0 + nn], bank.sl(0, nn), RSCALE, None, ALU.mult))
            proj_fm_stream(k_fm, wk_buf, lambda i: Wk[:, i * 128:(i + 1) * 128],
                           H, TS, kev, skey="wk")
            v_tok = emit_vproj(CCm, TS)
            def ogev(bank, i, n0, nn):
                pbread(b, bank, "s", lambda E, bank=bank, i=i, n0=n0, nn=nn: E.activation(
                    og_fm[:, i, n0:n0 + nn], bank.sl(0, nn), AF.Sigmoid,
                    bias=0.0, scale=1.0))
            proj_fm_stream(og_fm, wog_buf, lambda i: Wog[:, i * 128:(i + 1) * 128],
                           16, TS, ogev, skey="wg")
            tok_og = b.tok("s")
            if cfg.debug and s == 0:
                tfm = b.tok("s")
                b.wait("g", tfm)
                tdf = b.dma("g", dbg["dbg_fm"][:, :], xln_fm[:, 0, :], dkey())
                for _e in ("t", "v", "s"):
                    b.wait(_e, tdf)
            tk_la = emit_gproj(CCm, TS)
            if cfg.debug and s == 0:
                b.wait("y", tk_la)
                tda = b.dma("y", dbg["dbg_alpha"][:, :], alpha[:, :], dkey())
                for _e in ("t", "v", "s"):
                    b.wait(_e, tda)
            tkR, _ = emit_gateprep(CCm, tk_la)
            x_cons[0] = (b.tok("v"), b.tok("s"))
            b.wait("v", tok_og)
            if cfg.debug and s == 0:
                tkq = b.tok("v")
                b.wait("g", tkq)
                b.dma("g", dbg["dbg_q"][:, :], q_fm[:, 0, :], dkey())
                b.dma("g", dbg["dbg_k"][:, :], k_fm[:, 0, :], dkey())
                tdq = b.dma("g", dbg["dbg_v"][:, :], v_tm[:, 0, :], dkey())
                for _e in ("t", "v", "s"):
                    b.wait(_e, tdq)
                tog = b.tok("s")
                b.wait("g", tog)
                b.dma("g", dbg["dbg_og"][:, :], og_fm[:, 0, :], dkey())
                b.dma("g", dbg["dbg_la"][:, :], la[:, :], dkey())
                tdo = b.dma("g", dbg["dbg_b"][:, :], bsc[:, :], dkey())
                for _e in ("t", "v", "s"):
                    b.wait(_e, tdo)

            bank6, bank7 = pb[6], pb[7]
            for c in range(CCm):
                ci = s * CCm + c
                hst = {}

                def s1(h):
                    par = h % 2
                    bank4 = pb[4 + par]
                    b.wait("t", tkR)
                    b.wait("v", dpre_rd[par])
                    mm(b, bank6, bank6.sl(par * 256, 128),
                       isel8b[:, h * 128:(h + 1) * 128],
                       Rxb[:, c * C:(c + 1) * C], start=True, stop=True)
                    mm(b, bank6, bank6.sl(par * 256 + 128, 128),
                       isel8b[:, h * 128:(h + 1) * 128],
                       Rxm[:, c * C:(c + 1) * C], start=True, stop=True, seal=True)
                    mm(b, bank4, bank4.sl(384, 128), isel8f[:, h * 128:(h + 1) * 128],
                       bsc[:, c * C:(c + 1) * C], start=True, stop=True)
                    mm(b, bank4, bank4.sl(0, 128), k_fm[:, h, c * C:(c + 1) * C],
                       q_fm[:, h, c * C:(c + 1) * C], start=True, stop=True, seal=True)

                def s2(h):
                    par = h % 2
                    bank4 = pb[4 + par]
                    tdp = pbread(b, bank4, "v",
                                 lambda E, bank4=bank4, par=par, c=c, h=h: E.scalar_tensor_tensor(
                                     Dpre[par][:, :], bank4.sl(384, 128),
                                     btm[:, c * 8 + h:c * 8 + h + 1], imask[:, :],
                                     ALU.subtract, ALU.add))
                    b.wait("s", tdp)
                    tdt = b.opi("s", lambda E, par=par: E.activation(
                        Dt[par][:, :], Dpre[par][:, :], AF.Exp, bias=0.0, scale=1.0))
                    dpre_rd[par] = tdt
                    b.wait("v", tdt)
                    tsm = pbread(b, bank4, "v", lambda E, bank4=bank4, par=par: E.tensor_tensor(
                        smb[par][:, :], bank4.sl(0, 128), Dt[par][:, :], ALU.mult))
                    tqt = pbread(b, bank6, "v", lambda E, par=par, c=c, h=h: E.tensor_tensor(
                        qtb[par][:, :], q_fm[:, h, c * C:(c + 1) * C],
                        bank6.sl(par * 256, 128), ALU.mult))
                    b.wait("v", khb_rd[par])
                    tkh = pbread(b, bank6, "v", lambda E, par=par, c=c, h=h: E.tensor_tensor(
                        khb[par][:, :], k_fm[:, h, c * C:(c + 1) * C],
                        bank6.sl(par * 256 + 128, 128), ALU.mult))
                    tEB = pbread(b, bank6, "v", lambda E, par=par, h=h: E.tensor_copy(
                        expB_sb[:, h:h + 1], bank6.sl(par * 256 + 127, 1)))
                    tprev = b.tok("v")
                    b.wait("s", tprev)
                    b.wait("s", b.tok("t"))
                    tsb = b.opi("s", lambda E, par=par, h=h: E.copy(
                        Sbf[par][:, :], S_state[:, h * DV:(h + 1) * DV]))
                    hst[h] = (tsm, tqt, tkh, tEB, tsb)

                def s3(h):
                    par = h % 2
                    bank4 = pb[4 + par]
                    tsm, tqt, tkh, tEB, tsb = hst[h]
                    b.wait("t", tsm)
                    b.wait("t", tqt)
                    b.wait("t", tsb)
                    b.wait("t", v_tok)
                    mm(b, bank4, bank4.sl(128, 128), v_tm[:, c, h * DV:h * DV + 128],
                       smb[par][:, :], start=True, stop=False)
                    mm(b, bank4, bank4.sl(128, 128), Sbf[par][:, 0:128],
                       qtb[par][:, :], start=False, stop=True)
                    mm(b, bank4, bank4.sl(256, 128), v_tm[:, c, h * DV + 128:h * DV + 256],
                       smb[par][:, :], start=True, stop=False)
                    mm(b, bank4, bank4.sl(256, 128), Sbf[par][:, 128:256],
                       qtb[par][:, :], start=False, stop=True, seal=True)
                    b.wait("t", tkh)
                    mm(b, bank7, bank7.sl(par * 128, 128), khb[par][:, :],
                       iI128b[:, :], start=True, stop=True, seal=True)
                    khb_rd[par] = ("t", b.sem["t"][1])

                def s5(h):
                    par = h % 2
                    b.wait("t", hst[h][5])
                    mm(b, bank7, bank7.sl(256, 256), khT[par][:, :],
                       v_tm[:, c, h * DV:(h + 1) * DV], start=True, stop=True, seal=True)

                def s6(h):
                    par = h % 2
                    expB_rd[par] = pbread(
                        b, bank7, "v", lambda E, par=par, h=h: E.scalar_tensor_tensor(
                            S_state[:, h * DV:(h + 1) * DV],
                            S_state[:, h * DV:(h + 1) * DV],
                            expB_sb[:, h:h + 1], bank7.sl(256, 256),
                            ALU.mult, ALU.add))
                    last_stt[0] = expB_rd[par]

                def s4(h):
                    par = h % 2
                    bank4 = pb[4 + par]
                    pbread(b, bank4, "v", lambda E, bank4=bank4, c=c, h=h: E.tensor_tensor(
                        outog[:, 2 * h, :], bank4.sl(128, 128),
                        og_fm[:, 2 * h, c * C:(c + 1) * C], ALU.mult))
                    pbread(b, bank4, "v", lambda E, bank4=bank4, c=c, h=h: E.tensor_tensor(
                        outog[:, 2 * h + 1, :], bank4.sl(256, 128),
                        og_fm[:, 2 * h + 1, c * C:(c + 1) * C], ALU.mult))
                    b.wait("s", last_stt[0])
                    tkt = pbread(b, bank7, "s", lambda E, par=par: E.copy(
                        khT[par][:, :], bank7.sl(par * 128, 128)))
                    hst[h] = hst[h] + (tkt,)

                for i in range(H + 2):
                    if i < H:
                        s1(i)
                        s2(i)
                    if 1 <= i <= H:
                        s3(i - 1)
                    if i >= 2:
                        s5(i - 2)
                        s6(i - 2)
                    if 1 <= i <= H:
                        s4(i - 1)
                tog_done = b.tok("v")
                b.wait("t", tog_done)
                b.wait("t", tk_wo)
                for half in range(2):
                    bank = pb[half]
                    for kk in range(16):
                        mm(b, bank, bank.sl(0, 512), outog[:, kk, :],
                           Wo_sb[:, kk, half * 512:(half + 1) * 512],
                           start=(kk == 0), stop=(kk == 15), seal=(kk == 15))
                    pbread(b, bank, "s", lambda E, bank=bank, half=half: E.copy(
                        y_sb[:, half * 512:(half + 1) * 512], bank.t[:, 0:512]))
                tk_y = b.tok("s")
                if cfg.debug and ci == 0:
                    b.wait("y", tk_y)
                    tdm = b.dma("y", dbg["dbg_mix"][:, :], y_sb[:, :], dkey())
                    for _e in ("t", "v", "s"):
                        b.wait(_e, tdm)
                b.wait("v", tk_y)
                y3 = y_sb.rearrange("p (a n) -> p a n", a=1)
                emit_ln1(y3, 1, st_off=4)
                tyn = emit_norm_sub(y3, 0, yn_sb, st_off=4)
                b.wait("v", tyn)
                tx1 = b.opi("v", lambda E, c=c: E.tensor_tensor(
                    x_tm[:, c, :], x_tm[:, c, :], yn_sb[:, :], ALU.add))
                b.wait("y", tx1)
                x1_store[0] = b.dma(
                    "y", x1_sp.rearrange("(a p) n -> p a n", p=128)[:, ci, :],
                    x_tm[:, c, :], "st1")
        own_es.close()

    def emit_ffn():
        ffn_es = ExitStack()
        fsb = lambda name, shape, dt: ffn_es.enter_context(
            nc.sbuf_tensor(name, list(shape), dt))
        x1ln_fm = fsb("x1ln_fm", [128, 8, N_OWN], BF16)
        acc = fsb("acc", [128, n_tiles, D], F32)
        cw_bf = fsb("cw_bf", [128, n_tiles * NE], BF16)
        cwT64 = fsb("cwT64", [64, N_OWN], BF16)
        iota64 = fsb("iota64", [64, 1], F32)
        ones64 = fsb("ones64", [64, 128], BF16)
        selg = fsb("selg", [64, 4 * 128], BF16)
        selm = fsb("selm", [64, 1], F32)
        cwb_sb = fsb("cwb_sb", [128, 2, 384], BF16)
        h_sc = fsb("h_sc", [128, 2, 2, 384], BF16)
        scale_tok = [None, None]
        x1a = fsb("x1a", [128, D], F32)
        x1b = fsb("x1b", [128, D], F32)
        nmx = fsb("nmx", [128, n_tiles], F32)
        rec_sb = fsb("rec_sb", [128, n_tiles], F32)
        rw_sb = fsb("rw_sb", [128, 8, NE], BF16)
        rb_sb = fsb("rb_sb", [1, NE], BF16)
        b1T_sb = fsb("b1T_sb", [128, 2, NE], F32)
        sb1T_sb = fsb("sb1T_sb", [128, 16, 2], F32)
        b2_sb = fsb("b2_sb", [64, D], BF16)
        sb2_sb = fsb("sb2_sb", [2, D], BF16)
        f1_es = ExitStack()
        probs = f1_es.enter_context(nc.sbuf_tensor("probs", [128, n_tiles, NE], F32))
        pw = f1_es.enter_context(nc.sbuf_tensor("pw", [128, n_tiles, NE], F32))

        b.dma("y", iota64[:, :], c_iota64[:, :], "fc")
        b.dma("y", ones64[:, :], c_ones64b[:, :], "fc")
        b.dma("g", rw_sb[:, :, :], rw.rearrange("(a p) n -> p a n", p=128), "gc")
        b.dma("g", rb_sb[:, :], rb[:, :], "gc")
        b.dma("y", b1T_sb[:, :, :], b1T.rearrange("(a p) n -> p a n", p=128), "fc")
        b.dma("y", sb1T_sb[:, :, :], sb1T.rearrange("(a p) n -> p a n", p=128), "fc")
        b.dma("g", b2_sb[:, :], b2[:, :], "gc")
        tk_fc = b.dma("g", sb2_sb[:, :], sb2[:, :], "gc")
        tk_fy = ("fc", b.sem["fc"][1])
        tkacc = b.opi("v", lambda E: E.memset(acc[:, :, :], 0.0))

        x1bufs = [x1a, x1b]
        x1_toks = [None, None]
        if "st1" in b.sem:
            b.wait("y", ("st1", b.sem["st1"][1]))

        for ti in range(n_tiles):
            xb = x1bufs[ti % 2]
            if x1_toks[ti % 2] is not None:
                b.wait("y", x1_toks[ti % 2][0])
                b.wait("y", x1_toks[ti % 2][1])
            tkx = b.dma("y", xb[:, :],
                        x1_sp.rearrange("(a p) n -> p a n", p=128)[:, ti, :],
                        "fx%d" % (ti % 2))
            b.wait("v", tkx)
            b.wait("s", tkx)
            xb3 = xb.rearrange("p (a n) -> p a n", a=1)
            emit_ln1(xb3, 1, st_off=ti % 2)
            tkn = emit_norm_sub(xb3, 0, xlnbuf[ti % 2], st_off=ti % 2)
            x1_toks[ti % 2] = (b.tok("v"), b.tok("s"))
            emit_transpose_to_fm(xlnbuf[ti % 2], x1ln_fm, tkn, ti * 128)
            tk_fmd = b.tok("s")
            b.wait("t", tk_fmd)
            b.wait("t", tk_fc)
            b.wait("t", tk_fy)
            bank = pb[6]
            for kc in range(8):
                mm(b, bank, bank.sl(0, NE), x1ln_fm[:, kc, ti * 128:(ti + 1) * 128],
                   rw_sb[:, kc, :], start=(kc == 0), stop=False)
            mm(b, bank, bank.sl(0, NE), iones1b[:, :], rb_sb[:, :],
               start=False, stop=True, seal=True)
            tnm = pbread(b, bank, "v", lambda E, bank=bank, ti=ti: E.reduce_max(
                nmx[:, ti:ti + 1], bank.sl(0, NE), axis=AX.X, negate=True))
            b.wait("s", tnm)
            tpr = pbread(b, bank, "s", lambda E, bank=bank, ti=ti: E.activation(
                probs[:, ti, :], bank.sl(0, NE), AF.Exp,
                bias=nmx[:, ti:ti + 1], scale=1.0))
            b.wait("v", tpr)
            b.opi("v", lambda E, ti=ti: E.tensor_copy(pw[:, ti, :], probs[:, ti, :]))

        if cfg.debug:
            tv = b.tok("s")
            b.wait("y", tv)
            tdp_ = b.dma("y", dbg["dbg_probs"][:, :], probs[:, 0, :], dkey())
            for _e in ("t", "v", "s"):
                b.wait(_e, tdp_)

        def vbatch(mkfn):
            for ti in range(n_tiles):
                fn = mkfn(ti)
                if ti == n_tiles - 1:
                    b.opi("v", fn)
                else:
                    b.op("v", fn)
            b.vbounce()
        for r in range(TOPK):
            vbatch(lambda ti: lambda E, ti=ti: E.reduce_max(
                nmx[:, ti:ti + 1], pw[:, ti, :], axis=AX.X))
            if r < TOPK - 1:
                vbatch(lambda ti: lambda E, ti=ti: E.scalar_tensor_tensor(
                    pw[:, ti, :], pw[:, ti, :], nmx[:, ti:ti + 1],
                    pw[:, ti, :], ALU.is_lt, ALU.mult))
        vbatch(lambda ti: lambda E, ti=ti: E.scalar_tensor_tensor(
            pw[:, ti, :], probs[:, ti, :], nmx[:, ti:ti + 1],
            probs[:, ti, :], ALU.is_ge, ALU.mult))
        vbatch(lambda ti: lambda E, ti=ti: E.reduce_sum(
            nmx[:, ti:ti + 1], pw[:, ti, :], axis=AX.X))
        b.opi("v", lambda E: E.reciprocal(rec_sb[:, 0:n_tiles], nmx[:, 0:n_tiles]))
        b.vbounce()
        tlast = None
        for ti in range(n_tiles):
            fn = lambda E, ti=ti: E.tensor_scalar(
                pw[:, ti, :], pw[:, ti, :], rec_sb[:, ti:ti + 1], None, ALU.mult)
            tlast = b.opi("v", fn) if ti == n_tiles - 1 else b.op("v", fn)
        b.wait("s", tlast)
        tcw = b.opi("s", lambda E: E.copy(
            cw_bf[:, :], pw[:, :, :].rearrange("p a n -> p (a n)")))
        if cfg.debug:
            b.wait("y", tcw)
            tdc = b.dma("y", dbg["dbg_cw"][:, :], pw[:, 0, :], dkey())
            for _e in ("t", "v", "s"):
                b.wait(_e, tdc)
        b.wait("t", tcw)
        for ti in range(n_tiles):
            bank = pb[6]
            mm(b, bank, bank.sl(0, 128, 0, 64), cw_bf[:, ti * NE:(ti + 1) * NE],
               iI128b[:, :], start=True, stop=True, seal=True)
            pbread(b, bank, "s", lambda E, bank=bank, ti=ti: E.copy(
                cwT64[:, ti * 128:(ti + 1) * 128], bank.sl(0, 128, 0, 64)))
        tk_cwT = b.tok("s")

        if cfg.stop_after == "router":
            ffn_es.close()
            return
        b.barrier(("t", "v", "s", "y", "g"))
        f1_es.close()
        w1g = [fsb(f"w1g{i}", [128, 8, 4 * EH], BF16) for i in range(2)]
        w2g = [fsb(f"w2g{i}", [128, 2 * 4, D], BF16) for i in range(2)]

        groups = []
        for gi in range(16):
            groups.append([("r", gi * 4 + j) for j in range(4)])
        for gi in range(4):
            groups.append([("s", gi * 4 + j) for j in range(4)])

        def w_dma(slot, grp):
            toks = []
            for j, (kind, e) in enumerate(grp):
                sk = "gw%d" % slot
                if kind == "r":
                    t1 = b.dma("g", w1g[slot][:, :, j * EH:(j + 1) * EH],
                               w1[e].rearrange("(a p) n -> p a n", p=128), sk)
                    t2 = b.dma("g", w2g[slot][:, j * 2:(j + 1) * 2, :],
                               w2[e].rearrange("(a p) n -> p a n", p=128), sk)
                else:
                    si, ci_ = e // 8, e % 8
                    t1 = b.dma("g", w1g[slot][:, :, j * EH:(j + 1) * EH],
                               sw1[si][:, ci_ * EH:(ci_ + 1) * EH]
                               .rearrange("(a p) n -> p a n", p=128), sk)
                    t2 = b.dma("g", w2g[slot][:, j * 2:(j + 1) * 2, :],
                               sw2[si][ci_ * EH:(ci_ + 1) * EH, :]
                               .rearrange("(a p) n -> p a n", p=128), sk)
                toks.append((t1, t2))
            return toks[-1][1]

        wtoks = [None, None]
        wtoks[0] = w_dma(0, groups[0])
        if len(groups) > 1:
            wtoks[1] = w_dma(1, groups[1])

        for gidx, grp in enumerate(groups):
            slot = gidx % 2
            first_grp = (gidx == 0)
            ts0 = 0
            for tsi, tscnt in enumerate(ffn_ts):
                ntq = tscnt // 128
                if first_grp:
                    b.wait("t", tk_cwT)
                    b.wait("t", tk_fc)
                    for tq in range(ntq):
                        g128 = ts0 // 128 + tq
                        for half in range(2):
                            bank = pb[tq * 2 + half]
                            mm(b, bank, bank.sl(0, 512),
                               cwT64[:, g128 * 128:(g128 + 1) * 128],
                               b2_sb[:, half * 512:(half + 1) * 512],
                               start=True, stop=False)
                            mm(b, bank, bank.sl(0, 512), iones2b[:, :],
                               sb2_sb[:, half * 512:(half + 1) * 512],
                               start=False, stop=False)
                if tsi == 0 and grp[0][0] == "r":
                    b.wait("v", b.tok("t"))
                    for j, (kind, e) in enumerate(grp):
                        b.opi("v", lambda E, e=e: E.tensor_scalar(
                            selm[:, :], iota64[:, :], float(e), None, ALU.is_equal))
                        b.opi("v", lambda E, j=j: E.tensor_scalar(
                            selg[:, j * 128:(j + 1) * 128], ones64[:, :],
                            selm[:, :], None, ALU.mult))
                    tk_sel = b.tok("v")
                    b.wait("t", tk_sel)
                def front(j, kind, e):
                    par = j % 2
                    bankh = pb[6 + par]
                    b.wait("t", wtoks[slot])
                    if kind == "r":
                        b.wait("t", tk_cwT)
                        mm(b, bankh, bankh.sl(0, tscnt),
                           selg[:, j * 128:(j + 1) * 128],
                           cwT64[:, ts0:ts0 + tscnt], start=True, stop=True, seal=True)
                        b.wait("s", scale_tok[par])
                        pbread(b, bankh, "s",
                               lambda E, bankh=bankh, par=par, tscnt=tscnt: E.copy(
                                   cwb_sb[:, par, 0:tscnt], bankh.sl(0, tscnt)))
                    else:
                        b.wait("s", scale_tok[par])
                    for hc in range(2):
                        for kc in range(8):
                            mm(b, bankh, bankh.sl(0, tscnt),
                               w1g[slot][:, kc, j * EH + hc * 128:j * EH + (hc + 1) * 128],
                               x1ln_fm[:, kc, ts0:ts0 + tscnt],
                               start=(kc == 0), stop=(kc == 7), seal=(kc == 7))
                        if kind == "r":
                            bias_ap = b1T_sb[:, hc, e:e + 1]
                        else:
                            si, ci_ = e // 8, e % 8
                            bias_ap = sb1T_sb[:, ci_ * 2 + hc, si:si + 1]
                        dst = h_sc
                        pbread(b, bankh, "s",
                               lambda E, bankh=bankh, par=par, hc=hc, bias_ap=bias_ap,
                               dst=dst, tscnt=tscnt: E.activation(
                                   dst[:, par, hc, 0:tscnt], bankh.sl(0, tscnt),
                                   AF.Silu, bias=bias_ap, scale=1.0))
                    tk_h = b.tok("s")
                    if kind == "r":
                        b.wait("v", tk_h)
                        for hc in range(2):
                            b.op("v", lambda E, par=par, hc=hc, tscnt=tscnt: E.tensor_tensor(
                                h_sc[:, par, hc, 0:tscnt], h_sc[:, par, hc, 0:tscnt],
                                cwb_sb[:, par, 0:tscnt], ALU.mult))
                        scale_tok[par] = b.tok("v")
                    else:
                        scale_tok[par] = tk_h
                    return scale_tok[par]

                def back(j, kind, e, rdy):
                    par = j % 2
                    b.wait("t", rdy)
                    for tq in range(ntq):
                        for hc in range(2):
                            for half in range(2):
                                bank = pb[tq * 2 + half]
                                first_w = (j == 0 and hc == 0 and not first_grp)
                                last = (j == len(grp) - 1 and hc == 1)
                                mm(b, bank, bank.sl(0, 512),
                                   h_sc[:, par, hc, tq * 128:(tq + 1) * 128],
                                   w2g[slot][:, j * 2 + hc, half * 512:(half + 1) * 512],
                                   start=first_w, stop=last, seal=last)

                pend = None
                for j, (kind, e) in enumerate(grp):
                    rdy = front(j, kind, e)
                    if pend is not None:
                        back(*pend)
                    pend = (j, kind, e, rdy)
                back(*pend)
                for tq in range(ntq):
                    g128 = ts0 // 128 + tq
                    for half in range(2):
                        bank = pb[tq * 2 + half]
                        pbread(b, bank, "v",
                               lambda E, bank=bank, g128=g128, half=half: E.tensor_tensor(
                                   acc[:, g128, half * 512:(half + 1) * 512],
                                   acc[:, g128, half * 512:(half + 1) * 512],
                                   bank.t[:, 0:512], ALU.add))
                ts0 += tscnt
            if gidx + 2 < len(groups):
                b.wait("g", b.tok("t"))
                wtoks[slot] = w_dma(slot, groups[gidx + 2])
        tk_accv = b.opi("v", lambda E: E.tensor_copy(st_s1[:, 0:1], st_s1[:, 0:1]))
        if cfg.debug:
            b.wait("y", tk_accv)
            tdac = b.dma("y", dbg["dbg_acc"][:, :], acc[:, 0, :], dkey())
            for _e in ("t", "v", "s"):
                b.wait(_e, tdac)

        ydma = None
        for ti in range(n_tiles):
            xb = x1bufs[ti % 2]
            if ydma is not None:
                b.wait("v", ydma)
            if x1_toks[ti % 2] is not None:
                b.wait("y", x1_toks[ti % 2][0])
                b.wait("y", x1_toks[ti % 2][1])
            tkx = b.dma("y", xb[:, :],
                        x1_sp.rearrange("(a p) n -> p a n", p=128)[:, ti, :],
                        "fx%d" % (ti % 2))
            a3 = acc[:, ti, :].rearrange("p (a n) -> p a n", a=1)
            b.wait("v", tkacc)
            b.wait("s", tk_accv)
            emit_ln1(a3, 1, st_off=ti % 2)
            b.wait("s", ydma)
            tyn = emit_norm_sub(a3, 0, y_sb, st_off=ti % 2)
            b.wait("v", tyn)
            b.wait("v", tkx)
            tfin = b.opi("v", lambda E, xb=xb: E.tensor_tensor(
                y_sb[:, :], y_sb[:, :], xb[:, :], ALU.add))
            x1_toks[ti % 2] = (tfin, tfin)
            b.wait("y", tfin)
            ydma = b.dma("y", y_out.rearrange("(a p) n -> p a n", p=128)[:, ti, :],
                         y_sb[:, :], "sty")
        ffn_es.close()

    emit_prefix()
    pre_es.close()
    if cfg.stop_after != "prefix":
        b.barrier()
        emit_own()
    mix_es.close()
    if cfg.stop_after not in ("prefix", "mixer"):
        b.barrier()
        emit_ffn()

    b.barrier()

    with nc.Block() as block:
        @block.tensor
        def _(t):
            for fn in b.ops["t"]:
                fn(t)

        @block.vector
        def _(v):
            for fn in b.ops["v"]:
                fn(v)

        @block.scalar
        def _(s):
            for fn in b.ops["s"]:
                fn(s)

        @block.sync
        def _(y):
            for fn in b.ops["y"]:
                fn(y)

        @block.gpsimd
        def _(g):
            for fn in b.ops["g"]:
                fn(g)

    return nc, es, b



def make_consts():
    sel8 = np.zeros((8, 8 * 128), np.float32)
    for h in range(8):
        sel8[h, h * 128:(h + 1) * 128] = 1.0
    mask = np.where(np.arange(128)[:, None] <= np.arange(128)[None, :],
                    0.0, -1e30).astype(np.float32)
    return {
        "c_I128b": np.eye(128, dtype=np.float32).astype(BF),
        "c_I8f": np.eye(8, dtype=np.float32),
        "c_sel8f": sel8,
        "c_sel8b": sel8.astype(BF),
        "c_iota64": np.arange(64, dtype=np.float32).reshape(64, 1),
        "c_ones64b": np.ones((64, 128), np.float32).astype(BF),
        "c_mask": mask,
        "c_ones1b": np.ones((1, 128), np.float32).astype(BF),
        "c_ones2b": np.ones((2, 128), np.float32).astype(BF),
        "c_eps": np.full((128, 1), 1e-5, np.float32),
    }


def make_in_maps(inputs, cfg):
    x = np.ascontiguousarray(np.asarray(inputs["L0_attn_x"], dtype=np.float32))
    consts = make_consts()
    shared = {
        "Wq": np.ascontiguousarray(inputs["L0_attn_Wq"], dtype=np.float32),
        "Wk": np.ascontiguousarray(inputs["L0_attn_Wk"], dtype=np.float32),
        "Wv": np.ascontiguousarray(inputs["L0_attn_Wv"], dtype=np.float32),
        "Wo": np.ascontiguousarray(inputs["L0_attn_Wo"], dtype=np.float32),
        "Wg": np.ascontiguousarray(inputs["L0_attn_Wg"], dtype=np.float32),
        "bg": np.ascontiguousarray(
            np.asarray(inputs["L0_attn_bg"], dtype=np.float32).reshape(H, 1)),
        "Wog": np.ascontiguousarray(inputs["L0_attn_Wog"], dtype=np.float32),
        "rw": np.ascontiguousarray(inputs["L0_ffn_router_w"], dtype=np.float32),
        "rb": np.ascontiguousarray(
            np.asarray(inputs["L0_ffn_router_b"], dtype=np.float32).reshape(1, NE)),
        "w1": np.ascontiguousarray(inputs["L0_ffn_w1"], dtype=np.float32),
        "b1T": np.ascontiguousarray(
            np.asarray(inputs["L0_ffn_b1"], dtype=np.float32).T),
        "w2": np.ascontiguousarray(inputs["L0_ffn_w2"], dtype=np.float32),
        "b2": np.ascontiguousarray(inputs["L0_ffn_b2"], dtype=np.float32),
        "sw1": np.ascontiguousarray(inputs["L0_ffn_sw1"], dtype=np.float32),
        "sb1T": np.ascontiguousarray(
            np.asarray(inputs["L0_ffn_sb1"], dtype=np.float32).T),
        "sw2": np.ascontiguousarray(inputs["L0_ffn_sw2"], dtype=np.float32),
        "sb2": np.ascontiguousarray(inputs["L0_ffn_sb2"], dtype=np.float32),
    }
    shared.update(consts)
    in_maps = []
    for core in range(8):
        bi, r = core // 4, core % 4
        p = r * cfg.n_own
        xp = np.zeros((max(cfg.p_pre, 1), D), np.float32)
        take = min(p, cfg.p_pre)
        if take > 0:
            xp[cfg.p_pre - take:] = x[bi, p - take:p]
        m = dict(shared)
        m["x_pre"] = xp
        m["x_own"] = np.ascontiguousarray(x[bi, p:p + cfg.n_own])
        in_maps.append(m)
    return in_maps


_CFG = Cfg()
_CACHE = {}


def kernel(**inputs):
    cfg = _CFG
    key = (cfg.p_pre, cfg.n_own, cfg.stop_after, cfg.debug)
    if key not in _CACHE:
        _CACHE[key] = build_program(cfg)
    nc, _es = _CACHE[key]
    in_maps = make_in_maps(inputs, cfg)
    res = run_bass_kernel_spmd(nc, in_maps, core_ids=list(range(8)))
    B = np.asarray(inputs["L0_attn_x"]).shape[0]
    S = np.asarray(inputs["L0_attn_x"]).shape[1]
    y = np.empty((B, S, D), np.float32)
    for core in range(8):
        bi, r = core // 4, core % 4
        y[bi, r * cfg.n_own:(r + 1) * cfg.n_own] = res.results[core]["y"]
    return y
